# version 25
# baseline (speedup 1.0000x reference)
import contextlib
import numpy as np
import ml_dtypes
import concourse.bass as bass
import concourse.mybir as mybir
from concourse.bass_utils import run_bass_kernel_spmd

F32 = mybir.dt.float32
BF16 = mybir.dt.bfloat16
I16 = mybir.dt.int16
AX = mybir.AxisListType.X
ALU = mybir.AluOpType
AF = mybir.ActivationFunctionType

T = 2048
C = 1024
INNER = 4096
NT = 16
NG = 4
HEADS = 16
KV = 4
D = 64
EPS = 1e-5

STAGE = 3
RACES = False

ENGINES = ("tensor", "vector", "scalar", "gpsimd", "sync")


NLANE = 4


class Prog:

    def __init__(self):
        self.ops = []
        self.fence_cur = []
        self.qhist = {}

    def _add(self, engine, kind, emit, deps):
        deps = [d for d in deps if d is not None]
        if self.fence_cur:
            deps = deps + self.fence_cur
        oid = len(self.ops)
        self.ops.append({"engine": engine, "kind": kind, "emit": emit, "deps": deps})
        return oid

    def op(self, engine, emit, deps=()):
        return self._add(engine, "c", emit, list(deps))

    def dma(self, engine, out, in_, deps=(), **kw):
        def emit(eng, out=out, in_=in_, kw=kw):
            return eng.dma_start(out=out, in_=in_, **kw)
        oid = self._add(engine, "d", emit, list(deps))
        h = self.qhist.setdefault(engine, [])
        self.ops[oid]["lane"] = len(h) % NLANE
        if len(h) >= NLANE:
            self.ops[oid]["deps"].append(h[-NLANE])
        h.append(oid)
        return oid

    def qcover(self, engine):
        return list(self.qhist.get(engine, [])[-NLANE:])

    def fence(self):
        last = {}
        for i, o in enumerate(self.ops):
            if o["kind"] == "c":
                last[o["engine"]] = i
        snap = list(last.values())
        for q in self.qhist:
            snap += self.qhist[q][-NLANE:]
        self.fence_cur = snap

    def finalize(self, nc, block, sems, final_deps):
        ops = self.ops
        skey = {}
        for i, o in enumerate(ops):
            if o["kind"] == "d":
                skey[i] = (o["engine"], "d", o["lane"])
            else:
                skey[i] = (o["engine"], "c")
        needed = [o["kind"] == "d" for o in ops]
        for i, o in enumerate(ops):
            for d in o["deps"]:
                if not (skey[d] == skey[i] == ("tensor", "c")):
                    needed[d] = True
        for d in final_deps:
            needed[d] = True
        tokval = [0] * len(ops)
        counts = {}
        for i, o in enumerate(ops):
            k = skey[i]
            if needed[i]:
                counts[k] = counts.get(k, 0) + (16 if o["kind"] == "d" else 1)
                tokval[i] = counts[k]
        by_engine = {e: [] for e in ENGINES}
        for i, o in enumerate(ops):
            by_engine[o["engine"]].append(i)

        def run_for(name):
            def f(eng):
                wm = {}
                for i in by_engine[name]:
                    o = ops[i]
                    k = skey[i]
                    for d in o["deps"]:
                        dk = skey[d]
                        if dk == k and dk == ("tensor", "c"):
                            continue
                        v = tokval[d]
                        assert v > 0, f"dep {d} not inc'd"
                        if wm.get(dk, 0) < v:
                            eng.wait_ge(sems[dk], v)
                            wm[dk] = v
                    ins = o["emit"](eng)
                    if needed[i]:
                        ins.then_inc(sems[k], 16 if o["kind"] == "d" else 1)
                if name == "sync":
                    for d in final_deps:
                        dk = skey[d]
                        v = tokval[d]
                        if wm.get(dk, 0) < v:
                            eng.wait_ge(sems[dk], v)
                            wm[dk] = v
            return f

        block.sync(run_for("sync"))
        block.vector(run_for("vector"))
        block.scalar(run_for("scalar"))
        block.tensor(run_for("tensor"))
        block.gpsimd(run_for("gpsimd"))


class RW:

    def __init__(self):
        self.w = None
        self.r = []

    def rdeps(self):
        return list(self.w) if self.w else []

    def wdeps(self):
        return (list(self.w) if self.w else []) + self.r


def mk(P, engine, emit, reads=(), writes=(), extra=()):
    deps = list(extra)
    for b in reads:
        deps += b.rdeps()
    for b in writes:
        deps += b.wdeps()
    t = P.op(engine, emit, deps)
    for b in reads:
        b.r.append(t)
    for b in writes:
        b.w = [t]
        b.r = []
    return t


def mkdma(P, engine, out, in_, reads=(), writes=(), extra=(), **kw):
    deps = list(extra)
    for b in reads:
        deps += b.rdeps()
    for b in writes:
        deps += b.wdeps()
    t = P.dma(engine, out, in_, deps, **kw)
    for b in reads:
        b.r.append(t)
    for b in writes:
        b.w = [t]
        b.r = []
    return t


def build_program(stage=STAGE):
    nc = bass.Bass("TRN2", target_bir_lowering=False, detect_race_conditions=RACES)

    x_in = nc.dram_tensor("x", [T, C], F32, kind="ExternalInput")
    wq_in = nc.dram_tensor("wq", [C, C], F32, kind="ExternalInput")
    wk_in = nc.dram_tensor("wk", [KV * D, C], F32, kind="ExternalInput")
    wv_in = nc.dram_tensor("wv", [KV * D, C], F32, kind="ExternalInput")
    wp_in = nc.dram_tensor("wp", [C, C], F32, kind="ExternalInput")
    bp_in = nc.dram_tensor("bp", [1, C], F32, kind="ExternalInput")
    wg_in = nc.dram_tensor("wg", [INNER, C], F32, kind="ExternalInput")
    wu_in = nc.dram_tensor("wu", [INNER, C], F32, kind="ExternalInput")
    wd_in = nc.dram_tensor("wd", [C, INNER], F32, kind="ExternalInput")
    cones = nc.dram_tensor("cones", [128, 128], BF16, kind="ExternalInput")
    cident = nc.dram_tensor("cident", [128, 128], BF16, kind="ExternalInput")
    cmask = nc.dram_tensor("cmask", [128, 128], BF16, kind="ExternalInput")
    out_t = nc.dram_tensor("out", [T, C], F32, kind="ExternalOutput")

    tg_nat = nc.dram_tensor("tg_nat", [INNER, C], BF16)
    tu_nat = nc.dram_tensor("tu_nat", [INNER, C], BF16)
    td_nat = nc.dram_tensor("td_nat", [C, INNER], BF16)
    tgT = nc.dram_tensor("tgT", [C, INNER], BF16)
    tuT = nc.dram_tensor("tuT", [C, INNER], BF16)
    tdT = nc.dram_tensor("tdT", [INNER, C], BF16)
    q1h = nc.dram_tensor("q1h", [T, C], BF16)
    q2h = nc.dram_tensor("q2h", [T, INNER], BF16)
    x1h = nc.dram_tensor("x1h", [T, C], F32)
    x2h = nc.dram_tensor("x2h", [T, C], F32)
    ln2h = nc.dram_tensor("ln2h", [T, C], BF16)
    vh = nc.dram_tensor("vh", [KV * D, T], BF16)
    kh = nc.dram_tensor("kh", [KV * D, T], BF16)
    oh = nc.dram_tensor("oh", [T, C], BF16)
    wqbf = nc.dram_tensor("wqbf", [C, C], BF16)
    wkbf = nc.dram_tensor("wkbf", [KV * D, C], BF16)
    wvbf = nc.dram_tensor("wvbf", [KV * D, C], BF16)
    wpbf = nc.dram_tensor("wpbf", [C, C], BF16)

    P = Prog()

    with contextlib.ExitStack() as top:
        def sb(name, shape, dt=F32):
            return top.enter_context(nc.sbuf_tensor(name, shape, dt))

        sems = {}
        for e in ENGINES:
            sems[(e, "c")] = top.enter_context(nc.semaphore(f"c_{e}"))
            for l in range(NLANE):
                sems[(e, "d", l)] = top.enter_context(nc.semaphore(f"d_{e}{l}"))

        onesb = sb("onesb", [128, 128], BF16)
        maskb = sb("maskb", [128, 128], BF16)
        onesf = sb("onesf", [128, 1], F32)
        epst = sb("epst", [128, 1], F32)
        sc125 = sb("sc125", [128, 1], F32)
        bpbf = sb("bpbf", [1, C], BF16)
        b_bpbf = RW()
        meanv = {m: sb(f"meanv_{m}", [128, 1], F32) for m in "gud"}
        swv = {m: sb(f"swv_{m}", [128, 1], F32) for m in "gud"}

        b_onesb = RW(); b_maskb = RW(); b_onesf = RW(); b_epst = RW()
        b_sc125 = RW()
        b_meanv = {m: RW() for m in "gud"}
        b_swv = {m: RW() for m in "gud"}

        mkdma(P, "scalar", onesb[:], cones[:], writes=[b_onesb])
        identb = sb("identb", [128, 128], BF16)
        b_identb = RW()
        mkdma(P, "scalar", identb[:], cident[:], writes=[b_identb])
        mkdma(P, "scalar", maskb[:], cmask[:], writes=[b_maskb])
        mk(P, "vector", lambda e: e.memset(onesf[:], 1.0), writes=[b_onesf])
        mk(P, "vector", lambda e: e.memset(epst[:], EPS), writes=[b_epst])
        mk(P, "vector", lambda e: e.memset(sc125[:], 0.125), writes=[b_sc125])
        bp05 = sb("bp05", [128, 1], F32)
        bm05 = sb("bm05", [128, 1], F32)
        b_bp05 = RW(); b_bm05 = RW()
        mk(P, "vector", lambda e: e.memset(bp05[:], 0.5), writes=[b_bp05])
        mk(P, "vector", lambda e: e.memset(bm05[:], -0.5), writes=[b_bm05])
        xst = [sb(f"xst{i}", [128, C]) for i in range(2)]
        b_xst = [RW(), RW()]
        lnb = [sb(f"lnb{i}", [128, C]) for i in range(2)]
        b_lnb = [RW(), RW()]
        bnst = sb("bnst", [128, 12]); b_bnst = RW()
        mv = sb("mv", [128, 2]); b_mv = RW()
        rstd = sb("rstd", [128, 2]); b_rstd = RW()
        ast = sb("ast", [128, 4]); b_ast = RW()
        i0a = sb("i0a", [128, C], I16); b_i0a = RW()
        t1a = sb("t1a", [128, C], BF16); b_t1a = RW()
        fqa = sb("fqa", [128, C]); b_fqa = RW()
        qna1 = sb("qna", [128, C], BF16); b_qna1 = RW()
        qna = [qna1, qna1]; b_qna = [b_qna1, b_qna1]
        is1 = sb("is1", [128, NT]); b_is1 = [RW() for _ in range(NT)]
        vsv = sb("vsv", [128, NT]); b_vsv = [RW() for _ in range(NT)]
        mkdma(P, "scalar", xst[0][0:1, :], bp_in[:], writes=[b_xst[0]])
        mk(P, "scalar", lambda e: e.copy(bpbf[:], xst[0][0:1, :]), reads=[b_xst[0]], writes=[b_bpbf])

        ln_rot = [0]

        def emit_ln(src, tiles=None):
            t_q1h = []
            tl = list(tiles if tiles is not None else range(NT))
            slots = {}

            def load(i):
                sx = ln_rot[0] % 2
                ln_rot[0] += 1
                slots[i] = sx
                j = tl[i]
                mkdma(P, "gpsimd", xst[sx][:], src[j * 128:(j + 1) * 128, :],
                      writes=[b_xst[sx]])
            load(0)
            if len(tl) > 1:
                load(1)
            for i, j in enumerate(tl):
                sx = slots[i]
                s = j % 2
                mk(P, "vector", lambda e, sx=sx: e.bn_stats(bnst[:, 0:6], xst[sx][:, 0:512]),
                    reads=[b_xst[sx]], writes=[b_bnst])
                mk(P, "vector", lambda e, sx=sx: e.bn_stats(bnst[:, 6:12], xst[sx][:, 512:1024]),
                    reads=[b_xst[sx]], writes=[b_bnst])
                mk(P, "vector", lambda e: e.bn_aggr(mv[:], bnst[:]),
                    reads=[b_bnst], writes=[b_mv])
                mk(P, "vector", lambda e, j=j: e.tensor_single_scalar(
                    vsv[:, j:j + 1], mv[:, 1:2], 1.0, ALU.bypass),
                    reads=[b_mv], writes=[b_vsv[j]])
                mk(P, "vector", lambda e, s=s, sx=sx: e.tensor_single_scalar(
                    lnb[s][:], xst[sx][:], mv[:, 0:1], ALU.subtract),
                    reads=[b_xst[sx], b_mv], writes=[b_lnb[s]])
                if i + 2 < len(tl):
                    load(i + 2)
                mk(P, "vector", lambda e, s=s: e.tensor_reduce(
                    ast[:, 0:1], lnb[s][:], AX, ALU.max, apply_absolute_value=True),
                    reads=[b_lnb[s]], writes=[b_ast])
                mk(P, "vector", lambda e, j=j: e.tensor_single_scalar(
                    is1[:, j:j + 1], ast[:, 0:1], 1.0 / 127.0, ALU.mult),
                    reads=[b_ast], writes=[b_is1[j]])
                mk(P, "vector", lambda e: e.reciprocal(ast[:, 2:3], ast[:, 0:1]),
                    writes=[b_ast])
                mk(P, "vector", lambda e: e.tensor_single_scalar(
                    ast[:, 3:4], ast[:, 2:3], 127.0, ALU.mult),
                    writes=[b_ast])
                mk(P, "vector", lambda e, s=s: e.tensor_single_scalar(
                    i0a[:], lnb[s][:], ast[:, 3:4], ALU.mult),
                    reads=[b_lnb[s], b_ast], writes=[b_i0a])
                mk(P, "vector", lambda e, s=s: e.scalar_tensor_tensor(
                    fqa[:], lnb[s][:], ast[:, 3:4], i0a[:], ALU.mult, ALU.subtract),
                    reads=[b_lnb[s], b_ast, b_i0a], writes=[b_fqa])
                mk(P, "vector", lambda e: e.scalar_tensor_tensor(
                    t1a[:], fqa[:], 0.5, i0a[:], ALU.is_ge, ALU.add),
                    reads=[b_fqa, b_i0a], writes=[b_t1a])
                mk(P, "vector", lambda e, s=s: e.scalar_tensor_tensor(
                    qna[s][:], fqa[:], -0.5, t1a[:], ALU.is_le, ALU.subtract),
                    reads=[b_fqa, b_t1a], writes=[b_qna[s]])
                t_q1h.append(mkdma(P, "gpsimd", q1h[j * 128:(j + 1) * 128, :], qna[s][:], reads=[b_qna[s]]))
            return t_q1h

        def finish_is1(tiles=None):
            tt = list(tiles if tiles is not None else range(NT))
            for j in tt:
                mk(P, "scalar", lambda e, j=j: e.activation(
                    rstd[:, 0:1], vsv[:, j:j + 1], AF.Sqrt, bias=epst[:]),
                    reads=[b_vsv[j]], writes=[b_rstd], extra=b_epst.rdeps())
                mk(P, "vector", lambda e: e.reciprocal(rstd[:, 1:2], rstd[:, 0:1]),
                    writes=[b_rstd])
                mk(P, "vector", lambda e, j=j: e.tensor_tensor(
                    is1[:, j:j + 1], is1[:, j:j + 1], rstd[:, 1:2], ALU.mult),
                    reads=[b_rstd], writes=[b_is1[j]])


        st2q = top.enter_context(contextlib.ExitStack())

        def sbr(name, shape, dt=F32):
            return st2q.enter_context(nc.sbuf_tensor(name, shape, dt, side="right"))
        pmisc_rws = {m: RW() for m in "gud"}
        b_stg = [RW(), RW()]
        stg_rot = [0]
        pbuf = sbr("pbuf_q", [128, 2048], BF16)
        mbuf = sbr("mbuf_q", [128, 2048], BF16)
        tnb = mbuf
        tot = sbr("tot_q", [1, 4])
        onesrow = sbr("or_q", [1, 128])
        b_onesrow = RW(); b_pbuf = RW(); b_mbuf = RW()
        b_tnb = b_mbuf
        b_tot = {m: RW() for m in "gud"}
        mk(P, "vector", lambda e: e.memset(onesrow[:], 1.0), writes=[b_onesrow])
        partq = {m: sbr(f"part_{m}", [128, 1]) for m in "gud"}
        asumq = {m: sbr(f"asum_{m}", [128, 1]) for m in "gud"}
        stu = st2q.enter_context(contextlib.ExitStack())
        stg = [stu.enter_context(nc.psum_tensor(f"stg{i}", [128, 1024], F32))
               for i in range(2)]
        pmisc = stu.enter_context(nc.psum_tensor("pmisc", [128, 8], F32, side="right"))
        wstq = {"d": [stu.enter_context(nc.sbuf_tensor(f"wst_d{i}", [128, 1024], F32, side="right"))
                      for i in range(2)]}
        for m in "gu":
            wstq[m] = [stu.enter_context(nc.sbuf_tensor(f"wst_{m}{i}", [128, 2048], F32, side="right"))
                       for i in range(2)]
        ttW = {m: stu.enter_context(nc.sbuf_tensor(f"ttW_{m}", [128, 8, 1024], BF16, side="right"))
               for m in "gud"}
        b_wstq = {m: [RW(), RW()] for m in "gud"}
        b_partq = {m: RW() for m in "gud"}
        b_asumq = {m: RW() for m in "gud"}
        MI = {"g": 0, "u": 1, "d": 2}

        def quant_matrix(name, w_in, tT, rows, cols, route="act",
                         queue="gpsimd", wq="gpsimd", chunk_w=2048,
                         defer_p2=False):
            bw = min(cols, chunk_w)
            a = chunk_w // bw
            ncc = cols // bw
            nrb = rows // (128 * a)
            nblk = ncc * nrb
            mi = MI[name]
            wst = wstq[name]; b_wst = b_wstq[name]
            part = partq[name]; asum = asumq[name]
            b_part = b_partq[name]; b_asum = b_asumq[name]
            pmisc_rw = pmisc_rws[name]
            tw = ttW[name]

            def blocks():
                i = 0
                for hc in range(ncc):
                    for rb in range(nrb):
                        if a == 1:
                            ap = w_in[rb * 128:(rb + 1) * 128, hc * bw:(hc + 1) * bw]
                        else:
                            ap = w_in.rearrange("(b a p) c -> b p a c", a=a, p=128)[rb]
                        yield i, rb, hc, ap
                        i += 1

            mk(P, "vector", lambda e: e.memset(asum[:], 0.0), writes=[b_asum])
            for i, rb, hc, ap in blocks():
                s = i % 2
                dst = wst[s][:] if a == 1 else wst[s][:].rearrange("p (a c) -> p a c", a=a)
                mkdma(P, queue, dst, ap, writes=[b_wst[s]])
                if route == "act":
                    mk(P, "scalar", lambda e, s=s: e.activation(
                        pbuf[:, 0:chunk_w], wst[s][:], AF.Abs, accum_out=part[:]),
                        reads=[b_wst[s]], writes=[b_part, b_pbuf])
                else:
                    mk(P, "vector", lambda e, s=s: e.tensor_reduce(
                        part[:], wst[s][:], AX, ALU.add, apply_absolute_value=True),
                        reads=[b_wst[s]], writes=[b_part])
                mk(P, "vector", lambda e: e.tensor_tensor(asum[:], asum[:], part[:], ALU.add),
                    reads=[b_part], writes=[b_asum])
            c0, c1 = 2 * mi, 2 * mi + 1
            mk(P, "tensor", lambda e: e.matmul(
                pmisc[0:1, c0:c0 + 1], asum[:], onesf[:], start=True, stop=True),
                reads=[b_asum, b_onesf], writes=[pmisc_rw])
            mk(P, "vector", lambda e: e.tensor_scalar(
                tot[0:1, mi:mi + 1], pmisc[0:1, c0:c0 + 1], 1.0 / (rows * cols), 1e-5, ALU.mult, ALU.max),
                reads=[pmisc_rw], writes=[b_tot[name]])
            mk(P, "tensor", lambda e: e.matmul(
                pmisc[:, c1:c1 + 1], onesrow[:], tot[0:1, mi:mi + 1], start=True, stop=True),
                reads=[b_onesrow, b_tot[name]], writes=[pmisc_rw])
            mk(P, "vector", lambda e: e.reciprocal(swv[name][:], pmisc[:, c1:c1 + 1]),
                reads=[pmisc_rw], writes=[b_swv[name]])
            msc = 0.5 if route == "act" else 1.0
            mk(P, "vector", lambda e, msc=msc: e.tensor_single_scalar(
                meanv[name][:], pmisc[:, c1:c1 + 1], msc, ALU.mult),
                reads=[pmisc_rw], writes=[b_meanv[name]])

            def pass2():
                cnt = 0
                ev_tok = {0: [], 1: []}
                fl_tok = {0: [], 1: []}
                pend = []
                blist = list(blocks())
                evac_eng = "scalar"

                def emit_pend():
                    for fn in pend:
                        fn()
                    del pend[:]

                for bi, (i, rb, hc, ap) in enumerate(blist):
                    s = i % 2
                    dst = wst[s][:] if a == 1 else wst[s][:].rearrange("p (a c) -> p a c", a=a)
                    mkdma(P, queue, dst, ap, writes=[b_wst[s]])
                    emit_pend()
                    if route == "act":
                        mk(P, "scalar", lambda e, s=s: e.activation(
                            pbuf[:, 0:chunk_w], wst[s][:], AF.Sign, scale=swv[name][:], bias=bm05[:]),
                            reads=[b_wst[s], b_swv[name]], writes=[b_pbuf], extra=b_bm05.rdeps())
                        mk(P, "scalar", lambda e, s=s: e.activation(
                            mbuf[:, 0:chunk_w], wst[s][:], AF.Sign, scale=swv[name][:], bias=bp05[:]),
                            reads=[b_wst[s], b_swv[name]], writes=[b_mbuf], extra=b_bp05.rdeps())
                        mk(P, "vector", lambda e: e.scalar_tensor_tensor(
                            tnb[:, 0:chunk_w], pbuf[:, 0:chunk_w], -1.0, mbuf[:, 0:chunk_w],
                            ALU.mult, ALU.subtract),
                            reads=[b_pbuf], writes=[b_tnb])
                    else:
                        mk(P, "vector", lambda e, s=s: e.tensor_scalar(
                            pbuf[:, 0:chunk_w], wst[s][:], swv[name][:], 0.5, ALU.mult, ALU.is_ge),
                            reads=[b_wst[s], b_swv[name]], writes=[b_pbuf])
                        mk(P, "vector", lambda e, s=s: e.tensor_scalar(
                            mbuf[:, 0:chunk_w], wst[s][:], swv[name][:], -0.5, ALU.mult, ALU.is_le),
                            reads=[b_wst[s], b_swv[name]], writes=[b_mbuf])
                        mk(P, "vector", lambda e: e.tensor_tensor(
                            tnb[:, 0:chunk_w], mbuf[:, 0:chunk_w], pbuf[:, 0:chunk_w], ALU.subtract),
                            reads=[b_pbuf], writes=[b_tnb])
                    for a_ in range(a):
                        par = stg_rot[0] % 2
                        stg_rot[0] += 1
                        war = b_stg[par].wdeps()
                        for cb in range(8):
                            mk(P, "tensor", lambda e, par=par, a_=a_, cb=cb: e.matmul(
                                stg[par][:, cb * 128:(cb + 1) * 128],
                                tnb[:, a_ * bw + cb * 128:a_ * bw + (cb + 1) * 128],
                                identb[:], start=True, stop=True),
                                reads=[b_tnb], extra=(war + b_identb.rdeps() if cb == 0 else []))
                        b_stg[par].w = [len(P.ops) - 1]
                        b_stg[par].r = []
                        half = (cnt // 4) % 2
                        roff = half * 512 + (cnt % 4) * 128
                        flush = (cnt % 4 == 3)
                        rb_, hc_, i_, cnt_ = rb, hc, i, cnt

                        def do_ev(par=par, roff=roff, half=half, flush=flush,
                                  rb=rb_, hc=hc_, i=i_, a_=a_, cnt=cnt_):
                            if evac_eng == "scalar":
                                ev = mk(P, evac_eng, lambda e: e.copy(
                                    tw[:, :, roff:roff + 128],
                                    stg[par][:].rearrange("p (cb r) -> p cb r", cb=8)),
                                    reads=[b_stg[par]], extra=fl_tok[half])
                            else:
                                ev = mk(P, evac_eng, lambda e: e.tensor_copy(
                                    tw[:, :, roff:roff + 128],
                                    stg[par][:].rearrange("p (cb r) -> p cb r", cb=8)),
                                    reads=[b_stg[par]], extra=fl_tok[half])
                            ev_tok[half].append(ev)
                            if flush:
                                fl_tok[half] = []
                                for cb in range(8):
                                    if a == 1:
                                        pblk = hc * 8 + cb
                                        dstT = tT[pblk * 128:(pblk + 1) * 128,
                                                  (rb - 3) * 128:(rb + 1) * 128]
                                    else:
                                        rs = (cnt - 3) * 128
                                        dstT = tT[cb * 128:(cb + 1) * 128, rs:rs + 512]
                                    fl_tok[half].append(mkdma(
                                        P, wq, dstT, tw[:, cb, half * 512:half * 512 + 512],
                                        extra=ev_tok[half]))
                                ev_tok[half] = []
                        pend.append(do_ev)
                        cnt += 1
                emit_pend()
                return P.qcover(wq)
            if defer_p2:
                return pass2
            return pass2()

        def mid1():
            mst = contextlib.ExitStack()
            wstg = mst.enter_context(nc.sbuf_tensor("awst", [128, 512], F32, side="right"))
            wbfg = mst.enter_context(nc.sbuf_tensor("awbf", [128, 512], BF16, side="right"))
            b_wstg = RW(); b_wbfg = RW()
            for w_in_, wbf_, nrow in ((wq_in, wqbf, C), (wk_in, wkbf, KV * D),
                                      (wv_in, wvbf, KV * D), (wp_in, wpbf, C)):
                for t in range(nrow // 128):
                    for hf in range(2):
                        mkdma(P, "gpsimd", wstg[:], w_in_[t * 128:(t + 1) * 128, hf * 512:(hf + 1) * 512],
                              writes=[b_wstg])
                        mk(P, "gpsimd", lambda e: e.tensor_copy(wbfg[:], wstg[:]),
                            reads=[b_wstg], writes=[b_wbfg])
                        mkdma(P, "scalar", wbf_[t * 128:(t + 1) * 128, hf * 512:(hf + 1) * 512],
                              wbfg[:], reads=[b_wbfg])
            mst.close()

        d_p2 = quant_matrix("d", wd_in, tdT, C, INNER, queue="sync",
                            wq="scalar", chunk_w=1024, defer_p2=True)
        t_tgT = quant_matrix("g", wg_in, tgT, INNER, C, queue="gpsimd", wq="gpsimd")
        ln_toks = emit_ln(x_in, range(0, 4))
        finish_is1(range(0, 4))
        t_tuT = quant_matrix("u", wu_in, tuT, INNER, C, queue="sync", wq="sync")
        t_tdT = d_p2()
        mid1()
        stu.close()
        pmisc_guard = []
        for m in "gud":
            pmisc_guard += pmisc_rws[m].rdeps() + list(pmisc_rws[m].r)
        for bs in b_stg:
            pmisc_guard += bs.rdeps() + list(bs.r)
        for q in ("scalar", "gpsimd", "sync"):
            pmisc_guard += P.qcover(q)
        tern_done = {"g": t_tgT, "u": t_tuT, "d": t_tdT}

        def emit_ffn(call, src, dst, pre_q1=None, mid_cb=None, psum_guard=()):
            out_toks = []
            with contextlib.ExitStack() as st2:
                def sb2(name, shape, dt=F32):
                    return st2.enter_context(nc.sbuf_tensor(f"{name}_{call}", shape, dt))
                sgv = sb2("sgv", [128, NT])
                cvv = sb2("cvv", [128, NT])
                b_sgv = [RW() for _ in range(NT)]
                b_cvv = [RW() for _ in range(NT)]
                sepv = sb2("sepv", [128, NT])
                b_sepv = [RW() for _ in range(NT)]
                csv = sb2("csv", [128, NT])
                b_csv = [RW() for _ in range(NT)]
                q1t = sb2("q1t", [128, 8, 512], BF16)
                b_q1t = RW()
                wsl = [sb2(f"wsl{i}", [128, 8, 512], BF16) for i in range(3)]
                b_wsl = [RW(), RW(), RW()]
                sg1 = sb2("sg1", [128, 512])
                sg = [sg1, sg1]
                b_sg1 = RW()
                b_sg = [b_sg1, b_sg1]
                hb = sb2("hb", [128, 4, INNER])
                b_hb = [[RW() for _ in range(8)] for _ in range(4)]
                i0h = sb2("i0h", [128, INNER], I16)
                fqh = sb2("fqh", [128, INNER], BF16)
                t1h = sb2("t1h", [128, INNER], BF16)
                b_i0h = RW(); b_fqh = RW(); b_t1h = RW()
                q2t = sb2("q2t", [128, 8, 512], BF16)
                b_q2t = [RW() for _ in range(8)]
                wds = sb2("wds", [128, 2, 4, 1024], BF16)
                b_wds = [RW(), RW()]
                xout = sb2("xout", [128, 2, C])
                b_xout = [RW(), RW()]
                pd = [st2.enter_context(nc.psum_tensor(f"pd{i}_{call}", [128, 512], F32)) for i in range(4)]
                pg = [st2.enter_context(nc.psum_tensor(f"pg{i}_{call}", [128, 512], F32)) for i in range(2)]
                pu = [st2.enter_context(nc.psum_tensor(f"pu{i}_{call}", [128, 512], F32)) for i in range(2)]
                b_pg = [RW(), RW()]; b_pu = [RW(), RW()]; b_pd = [RW(), RW(), RW(), RW()]
                for rwb in (b_pg + b_pu + b_pd + b_sgv + b_cvv + b_sepv + b_csv
                            + [b_q1t, b_sg1, b_i0h, b_fqh, b_t1h] + b_wsl
                            + [x for row in b_hb for x in row]
                            + b_q2t + list(b_wds) + list(b_xout)):
                    rwb.r += list(psum_guard)

                if pre_q1 is None:
                    t_q1h = emit_ln(src)
                    finish_is1()
                else:
                    t_q1h = pre_q1

                tgT_r = tgT.rearrange("(c p) i -> p c i", p=128)
                tuT_r = tuT.rearrange("(c p) i -> p c i", p=128)
                tdT_r = tdT.rearrange("(k p) c -> p k c", p=128)
                q1t_tok = None

                t_q2_done = {}

                def phase1(g):
                    q2toks = []
                    gs, ge = g * 512, (g + 1) * 512
                    for j in range(g * 4, (g + 1) * 4):
                        mk(P, "vector", lambda e, j=j: e.tensor_tensor(
                            sgv[:, j:j + 1], is1[:, j:j + 1], meanv["g"][:], ALU.mult),
                            reads=[b_is1[j], b_meanv["g"]], writes=[b_sgv[j]])
                        mk(P, "vector", lambda e, j=j: e.tensor_tensor(
                            cvv[:, j:j + 1], is1[:, j:j + 1], meanv["u"][:], ALU.mult),
                            reads=[b_is1[j], b_meanv["u"]], writes=[b_cvv[j]])
                    q1t_war = b_q1t.wdeps()
                    for f in range(8):
                        mkdma(P, "sync", q1t[:, f, :],
                              q1h[gs:ge, f * 128:(f + 1) * 128],
                              extra=q1t_war + t_q1h[g * 4:(g + 1) * 4], transpose=True)
                    b_q1t.w = P.qcover("sync")
                    b_q1t.r = []

                    for n in range(8):
                        sg_slot = (2 * n) % 3
                        su_slot = (2 * n + 1) % 3
                        mkdma(P, "gpsimd", wsl[sg_slot][:], tgT_r[:, :, n * 512:(n + 1) * 512],
                              writes=[b_wsl[sg_slot]], extra=tern_done["g"])
                        mkdma(P, "gpsimd", wsl[su_slot][:], tuT_r[:, :, n * 512:(n + 1) * 512],
                              writes=[b_wsl[su_slot]], extra=tern_done["u"])
                        for j4 in range(4):
                            j = g * 4 + j4
                            sp = j4 % 2
                            for f in range(8):
                                mk(P, "tensor", lambda e, f=f, sp=sp, j4=j4, s=sg_slot: e.matmul(
                                    pg[sp][:], q1t[:, f, j4 * 128:(j4 + 1) * 128],
                                    wsl[s][:, f, :], start=(f == 0), stop=(f == 7)),
                                    reads=[b_q1t, b_wsl[sg_slot]],
                                    writes=[b_pg[sp]] if f == 0 else [],
                                    extra=([] if f else b_pg[sp].wdeps()))
                                if f == 7:
                                    b_pg[sp].w = [len(P.ops) - 1]
                                    b_pg[sp].r = []
                            mk(P, "scalar", lambda e, sp=sp, j=j: e.activation(
                                sg[sp][:], pg[sp][:], AF.Silu, scale=sgv[:, j:j + 1]),
                                reads=[b_pg[sp], b_sgv[j]], writes=[b_sg[sp]])
                            for f in range(8):
                                mk(P, "tensor", lambda e, f=f, sp=sp, j4=j4, s=su_slot: e.matmul(
                                    pu[sp][:], q1t[:, f, j4 * 128:(j4 + 1) * 128],
                                    wsl[s][:, f, :], start=(f == 0), stop=(f == 7)),
                                    reads=[b_q1t, b_wsl[su_slot]],
                                    writes=[b_pu[sp]] if f == 0 else [],
                                    extra=([] if f else b_pu[sp].wdeps()))
                                if f == 7:
                                    b_pu[sp].w = [len(P.ops) - 1]
                                    b_pu[sp].r = []
                            mk(P, "vector", lambda e, sp=sp, j4=j4, n=n: e.tensor_tensor(
                                hb[:, j4, n * 512:(n + 1) * 512], sg[sp][:], pu[sp][:], ALU.mult),
                                reads=[b_sg[sp], b_pu[sp]], writes=[b_hb[j4][n]])

                    for j4 in range(4):
                        j = g * 4 + j4
                        s = j4 % 2
                        hrw = b_hb[j4]
                        hdeps = [t for x in hrw for t in (x.w or [])]
                        mk(P, "vector", lambda e, j4=j4: e.tensor_reduce(
                            ast[:, 0:1], hb[:, j4, :], AX, ALU.max, apply_absolute_value=True),
                            writes=[b_ast], extra=hdeps)
                        mk(P, "vector", lambda e, j=j: e.tensor_tensor(
                            ast[:, 1:2], ast[:, 0:1], cvv[:, j:j + 1], ALU.mult),
                            reads=[b_cvv[j]], writes=[b_ast])
                        mk(P, "vector", lambda e: e.tensor_single_scalar(
                            ast[:, 1:2], ast[:, 1:2], 1e-5, ALU.max),
                            writes=[b_ast])
                        mk(P, "vector", lambda e: e.reciprocal(ast[:, 2:3], ast[:, 1:2]),
                            writes=[b_ast])
                        mk(P, "vector", lambda e, j=j: e.tensor_single_scalar(
                            ast[:, 3:4], ast[:, 2:3], 127.0, ALU.mult),
                            writes=[b_ast])
                        mk(P, "vector", lambda e, j=j: e.tensor_tensor(
                            csv[:, j:j + 1], ast[:, 3:4], cvv[:, j:j + 1], ALU.mult),
                            reads=[b_ast, b_cvv[j]], writes=[b_csv[j]])
                        mk(P, "vector", lambda e, j=j: e.tensor_single_scalar(
                            ast[:, 1:2], ast[:, 1:2], 0.5 / 127.0, ALU.mult),
                            writes=[b_ast])
                        mk(P, "vector", lambda e, j=j: e.tensor_tensor(
                            sepv[:, j:j + 1], ast[:, 1:2], meanv["d"][:], ALU.mult),
                            reads=[b_ast, b_meanv["d"]], writes=[b_sepv[j]])
                        mk(P, "vector", lambda e, j4=j4, j=j: e.tensor_single_scalar(
                            i0h[:], hb[:, j4, :], csv[:, j:j + 1], ALU.mult),
                            reads=[b_csv[j]], writes=[b_i0h], extra=hdeps)
                        mk(P, "vector", lambda e, j4=j4, j=j: e.scalar_tensor_tensor(
                            fqh[:], hb[:, j4, :], csv[:, j:j + 1], i0h[:], ALU.mult, ALU.subtract),
                            reads=[b_csv[j], b_i0h], writes=[b_fqh], extra=hdeps)
                        mk(P, "vector", lambda e: e.scalar_tensor_tensor(
                            t1h[:], fqh[:], 0.5, i0h[:], ALU.is_ge, ALU.add),
                            reads=[b_fqh, b_i0h], writes=[b_t1h])
                        if call == 2:
                            mk(P, "vector", lambda e: e.scalar_tensor_tensor(
                                t1h[:], fqh[:], 1.5, t1h[:], ALU.is_ge, ALU.add),
                                reads=[b_fqh], writes=[b_t1h])
                            mk(P, "vector", lambda e: e.scalar_tensor_tensor(
                                t1h[:], fqh[:], -0.5, t1h[:], ALU.is_le, ALU.subtract),
                                reads=[b_fqh], writes=[b_t1h])
                            hfree = mk(P, "vector", lambda e: e.scalar_tensor_tensor(
                                fqh[:], fqh[:], -1.5, t1h[:], ALU.is_le, ALU.add),
                                reads=[b_t1h], writes=[b_fqh])
                        else:
                            hfree = mk(P, "vector", lambda e: e.scalar_tensor_tensor(
                                fqh[:], fqh[:], -0.5, t1h[:], ALU.is_le, ALU.subtract),
                                reads=[b_t1h], writes=[b_fqh])
                        for x in hrw:
                            x.r.append(hfree)
                        q2toks.append(mkdma(P, "scalar", q2h[j * 128:(j + 1) * 128, :], fqh[:], reads=[b_fqh]))
                    t_q2_done[g] = list(q2toks)

                def phase2(g):
                    gs, ge = g * 512, (g + 1) * 512
                    q2cover = t_q2_done[g]
                    dbanks = [pd[0], pd[1], pd[2], pd[3], pg[0], pg[1], pu[0], pu[1]]
                    dbrw = [b_pd[0], b_pd[1], b_pd[2], b_pd[3], b_pg[0], b_pg[1], b_pu[0], b_pu[1]]
                    for ha in range(8):
                        ws = ha % 2
                        for k4 in range(4):
                            w = ha * 4 + k4
                            mkdma(P, "sync",
                                  q2t[:, w % 8, :], q2h[gs:ge, w * 128:(w + 1) * 128],
                                  writes=[b_q2t[w % 8]], extra=q2cover, transpose=True)
                        mkdma(P, "gpsimd", wds[:, ws, :, :], tdT_r[:, ha * 4:(ha + 1) * 4, :],
                              writes=[b_wds[ws]], extra=tern_done["d"])
                        for j4 in range(4):
                            for f2 in range(2):
                                pdi = j4 * 2 + f2
                                for k4 in range(4):
                                    w = ha * 4 + k4
                                    mk(P, "tensor", lambda e, w=w, k4=k4, ws=ws, j4=j4, f2=f2, pdi=pdi, ha=ha: e.matmul(
                                        dbanks[pdi][:], q2t[:, w % 8, j4 * 128:(j4 + 1) * 128],
                                        wds[:, ws, k4, f2 * 512:(f2 + 1) * 512],
                                        start=(ha == 0 and k4 == 0), stop=(ha == 7 and k4 == 3)),
                                        reads=[b_q2t[w % 8], b_wds[ws]],
                                        writes=[dbrw[pdi]] if (ha == 0 and k4 == 0) else [],
                                        extra=([] if (ha or k4) else dbrw[pdi].wdeps()))
                                    if ha == 7 and k4 == 3:
                                        dbrw[pdi].w = [len(P.ops) - 1]
                                        dbrw[pdi].r = []
                    for j4 in range(4):
                        j = g * 4 + j4
                        s = j4 % 2
                        mkdma(P, "gpsimd", xst[s][:], src[j * 128:(j + 1) * 128, :], writes=[b_xst[s]])
                        for f2 in range(2):
                            pdi = j4 * 2 + f2
                            mk(P, "vector", lambda e, j4=j4, f2=f2, j=j, s=s, pdi=pdi: e.scalar_tensor_tensor(
                                xout[:, j4 % 2, f2 * 512:(f2 + 1) * 512],
                                dbanks[pdi][:],
                                sepv[:, j:j + 1],
                                xst[s][:, f2 * 512:(f2 + 1) * 512], ALU.mult, ALU.add),
                                reads=[dbrw[pdi], b_sepv[j], b_xst[s]], writes=[b_xout[j4 % 2]])
                        out_toks.append(mkdma(P, "scalar", dst[j * 128:(j + 1) * 128, :],
                                              xout[:, j4 % 2, :], reads=[b_xout[j4 % 2]]))

                phase1(0)
                if mid_cb:
                    mid_cb(0)
                phase1(1)
                if mid_cb:
                    mid_cb(1)
                phase1(2)
                if mid_cb:
                    mid_cb(2)
                for g in range(NG):
                    if g + 3 < NG:
                        phase1(g + 3)
                    phase2(g)
            return out_toks

        def emit_attn(src, dst):
            a_out_toks = []
            with contextlib.ExitStack() as st2:
                def sb2(name, shape, dt=F32):
                    return st2.enter_context(nc.sbuf_tensor(name, shape, dt))
                lnbf1 = sb2("alnbf", [128, C], BF16)
                lnbf = [lnbf1, lnbf1]
                b_lnbf1 = RW()
                b_lnbf = [b_lnbf1, b_lnbf1]
                wpT = sb2("wpT", [128, 8, C], BF16); b_wpT = RW()
                rz = sb2("rz", [128, 1]); b_rz = RW()
                xob1 = sb2("axob", [128, C])
                xob = [xob1, xob1]
                b_xob1 = RW()
                b_xob = [b_xob1, b_xob1]
                stA = st2.enter_context(contextlib.ExitStack())
                def sbA(name, shape, dt=F32):
                    return stA.enter_context(nc.sbuf_tensor(name, shape, dt))
                ln2t = [sbA(f"ln2t{i}", [128, 8, 512], BF16) for i in range(2)]
                b_ln2t = [RW(), RW()]
                wqT = sbA("wqT", [128, 8, C], BF16); b_wqT = RW()
                qfm = sbA("qfm", [128, 8, T], BF16); b_qfm = RW()
                kfm2 = sbA("kfm2", [128, 4, T], BF16); b_kfm = RW()
                vstg = lnbf1[:, 0:512]; b_vstg = b_lnbf1
                vt65 = sbA("vt65", [128, 16, 260], BF16); b_vt65 = RW()
                strips = sbA("strips", [128, 16, 512], BF16)
                b_str = [RW() for _ in range(16)]
                otm = sbA("otm", [128, 16, C], BF16)
                b_otm = RW()
                stB = stA.enter_context(contextlib.ExitStack())
                wkT = stB.enter_context(nc.sbuf_tensor("wkT", [128, 8, KV * D], BF16)); b_wkT = RW()
                wvT = stB.enter_context(nc.sbuf_tensor("wvT", [128, 8, KV * D], BF16)); b_wvT = RW()

                pp = [st2.enter_context(nc.psum_tensor(f"app{i}", [128, 1024], F32)) for i in range(2)]
                b_pp = [RW(), RW()]
                pv = [st2.enter_context(nc.psum_tensor(f"apv{i}", [128, 512], F32)) for i in range(4)]
                b_pv = [RW(), RW(), RW(), RW()]

                for j in range(NT):
                    s = j % 2
                    mkdma(P, "scalar", xst[s][:], src[j * 128:(j + 1) * 128, :], writes=[b_xst[s]])
                    mk(P, "vector", lambda e, s=s: e.bn_stats(bnst[:, 0:6], xst[s][:, 0:512]),
                        reads=[b_xst[s]], writes=[b_bnst])
                    mk(P, "vector", lambda e, s=s: e.bn_stats(bnst[:, 6:12], xst[s][:, 512:1024]),
                        reads=[b_xst[s]], writes=[b_bnst])
                    mk(P, "vector", lambda e: e.bn_aggr(mv[:], bnst[:]),
                        reads=[b_bnst], writes=[b_mv])
                    mk(P, "scalar", lambda e: e.activation(rstd[:, 0:1], mv[:, 1:2], AF.Sqrt, bias=epst[:]),
                        reads=[b_mv], writes=[b_rstd], extra=b_epst.rdeps())
                    mk(P, "vector", lambda e: e.reciprocal(rstd[:, 1:2], rstd[:, 0:1]),
                        writes=[b_rstd])
                    mk(P, "vector", lambda e, s=s: e.tensor_scalar(
                        lnbf[s][:], xst[s][:], mv[:, 0:1], rstd[:, 1:2], ALU.subtract, ALU.mult),
                        reads=[b_xst[s], b_mv, b_rstd], writes=[b_lnbf[s]])
                    mkdma(P, "scalar", ln2h[j * 128:(j + 1) * 128, :], lnbf[s][:], reads=[b_lnbf[s]])
                ln2_cover = P.qcover("scalar")

                for buf, brw, hbmsrc, nr in ((wqT, b_wqT, wqbf, C), (wkT, b_wkT, wkbf, KV * D),
                                             (wvT, b_wvT, wvbf, KV * D), (wpT, b_wpT, wpbf, C)):
                    for f in range(8):
                        mkdma(P, "sync", buf[:, f, 0:nr], hbmsrc[:, f * 128:(f + 1) * 128],
                              transpose=True)
                    brw.w = P.qcover("sync")
                    brw.r = []

                for tb in range(4):
                    lb = tb % 2
                    ln2t_war = b_ln2t[lb].wdeps()
                    for f in range(8):
                        mkdma(P, "sync", ln2t[lb][:, f, :],
                              ln2h[tb * 512:(tb + 1) * 512, f * 128:(f + 1) * 128],
                              extra=ln2t_war + ln2_cover, transpose=True)
                    b_ln2t[lb].w = P.qcover("sync")
                    b_ln2t[lb].r = []
                    kk = 0
                    for qdst, wT, b_wT, nm in ((qfm, wqT, b_wqT, 8), (kh, wkT, b_wkT, 2), (vh, wvT, b_wvT, 2)):
                        for m in range(nm):
                            sp = kk % 2
                            kk += 1
                            for f in range(8):
                                mk(P, "tensor", lambda e, m=m, f=f, sp=sp, wT=wT, lb=lb: e.matmul(
                                    pp[sp][:, 0:512], wT[:, f, m * 128:(m + 1) * 128],
                                    ln2t[lb][:, f, :],
                                    start=(f == 0), stop=(f == 7)),
                                    reads=[b_wT, b_ln2t[lb]],
                                    writes=[b_pp[sp]] if f == 0 else [],
                                    extra=([] if f else b_pp[sp].wdeps()))
                                if f == 7:
                                    b_pp[sp].w = [len(P.ops) - 1]
                                    b_pp[sp].r = []
                            if qdst is qfm:
                                mk(P, "vector", lambda e, m=m, tb=tb, sp=sp: e.tensor_copy(
                                    qfm[:, m, tb * 512:(tb + 1) * 512], pp[sp][:, 0:512]),
                                    reads=[b_pp[sp]], writes=[b_qfm])
                            else:
                                mk(P, "vector", lambda e, sp=sp: e.tensor_copy(vstg, pp[sp][:, 0:512]),
                                    reads=[b_pp[sp]], writes=[b_vstg])
                                mkdma(P, "scalar", qdst[m * 128:(m + 1) * 128, tb * 512:(tb + 1) * 512],
                                      vstg, reads=[b_vstg])
                kv_cover = P.qcover("scalar")
                kfm_war = b_kfm.wdeps()
                for g2 in range(KV):
                    for off in range(2):
                        mkdma(P, "gpsimd", kfm2[off * 64:(off + 1) * 64, g2, :],
                              kh[g2 * 64:(g2 + 1) * 64, :], extra=kfm_war + kv_cover)
                b_kfm.w = P.qcover("gpsimd")
                b_kfm.r = []

                stB.close()
                with nc.sbuf_tensor("vtm4", [128, 16, 256], BF16) as vtm4:
                    b_vtm4 = RW()
                    for t in range(16):
                        mkdma(P, "sync", vtm4[:, t, :], vh[:, t * 128:(t + 1) * 128],
                              extra=kv_cover, transpose=True)
                    b_vtm4.w = P.qcover("sync")
                    b_vtm4.r = []
                    mk(P, "vector", lambda e: e.memset(vt65[:], 1.0), writes=[b_vt65])
                    for t in range(16):
                        for g2 in range(4):
                            mk(P, "vector", lambda e, t=t, g2=g2: e.tensor_copy(
                                vt65[:, t, g2 * 65:g2 * 65 + 64], vtm4[:, t, g2 * 64:(g2 + 1) * 64]),
                                reads=[b_vtm4], writes=[b_vt65])

                for h in range(HEADS):
                    g2 = h // 4
                    qch, qoff = h // 2, 64 * (h % 2)
                    for J in range(4):
                        ntile = 4 * J + 4
                        for pi in range(ntile // 2):
                            sp = pi % 2
                            for half in range(2):
                                t = pi * 2 + half
                                mk(P, "tensor", lambda e, t=t, half=half, sp=sp, J=J, g2=g2, qch=qch, qoff=qoff: e.matmul(
                                    pp[sp][:, half * 512:(half + 1) * 512],
                                    kfm2[qoff:qoff + 64, g2, t * 128:(t + 1) * 128],
                                    qfm[qoff:qoff + 64, qch, J * 512:(J + 1) * 512],
                                    start=True, stop=True),
                                    reads=[b_kfm, b_qfm],
                                    writes=[b_pp[sp]] if half == 0 else [],
                                    extra=([] if half else b_pp[sp].wdeps()))
                                if half == 1:
                                    b_pp[sp].w = [len(P.ops) - 1]
                                    b_pp[sp].r = []
                            t0 = pi * 2
                            texp = mk(P, "scalar", lambda e, t0=t0, sp=sp: e.activation(
                                strips[:, t0:t0 + 2, :], pp[sp][:], AF.Exp, scale=sc125[:]),
                                reads=[b_pp[sp], b_sc125],
                                writes=[b_str[t0], b_str[t0 + 1]])
                        for t in range(4 * J, 4 * J + 4):
                            p = t - 4 * J
                            mk(P, "vector", lambda e, t=t, p=p: e.tensor_tensor(
                                strips[:, t, p * 128:(p + 1) * 128],
                                strips[:, t, p * 128:(p + 1) * 128], maskb[:], ALU.mult),
                                reads=[b_maskb], writes=[b_str[t]])
                        for s4 in range(4):
                            jj = J * 4 + s4
                            sp = s4
                            nk = 4 * J + s4 + 1
                            for t in range(nk):
                                mk(P, "tensor", lambda e, t=t, s4=s4, sp=sp, g2=g2, nk=nk: e.matmul(
                                    pv[sp][:, 0:65], strips[:, t, s4 * 128:(s4 + 1) * 128],
                                    vt65[:, t, g2 * 65:(g2 + 1) * 65],
                                    start=(t == 0), stop=(t == nk - 1)),
                                    reads=[b_str[t], b_vt65],
                                    writes=[b_pv[sp]] if t == 0 else [],
                                    extra=([] if t else b_pv[sp].wdeps()))
                                if t == nk - 1:
                                    b_pv[sp].w = [len(P.ops) - 1]
                                    b_pv[sp].r = []
                            mk(P, "vector", lambda e, sp=sp: e.reciprocal(rz[:], pv[sp][:, 64:65]),
                                reads=[b_pv[sp]], writes=[b_rz])
                            mk(P, "vector", lambda e, jj=jj, h=h, sp=sp: e.tensor_single_scalar(
                                otm[:, jj, h * 64:(h + 1) * 64], pv[sp][:, 0:64], rz[:],
                                ALU.mult),
                                reads=[b_pv[sp], b_rz], writes=[b_otm])

                for j in range(NT):
                    mkdma(P, "scalar", oh[j * 128:(j + 1) * 128, :], otm[:, j, :], reads=[b_otm])
                oh_cover = P.qcover("scalar")
                stA.close()
                oT = sb2("oT", [128, 8, T], BF16)
                b_oT = RW()
                for f in range(8):
                    mkdma(P, "sync", oT[:, f, :], oh[:, f * 128:(f + 1) * 128],
                          extra=oh_cover, transpose=True)
                b_oT.w = P.qcover("sync")
                b_oT.r = []
                for j in range(NT):
                    s = j % 2
                    mkdma(P, "scalar", xst[s][:], src[j * 128:(j + 1) * 128, :], writes=[b_xst[s]])
                    for f2 in range(2):
                        sp = f2
                        for f in range(8):
                            mk(P, "tensor", lambda e, j=j, f=f, f2=f2, sp=sp: e.matmul(
                                pv[sp][:], oT[:, f, j * 128:(j + 1) * 128],
                                wpT[:, f, f2 * 512:(f2 + 1) * 512],
                                start=(f == 0), stop=False),
                                reads=[b_oT, b_wpT],
                                writes=[b_pv[sp]] if f == 0 else [],
                                extra=([] if f else b_pv[sp].wdeps()))
                        mk(P, "tensor", lambda e, f2=f2, sp=sp: e.matmul(
                            pv[sp][:], onesb[0:1, :], bpbf[0:1, f2 * 512:(f2 + 1) * 512],
                            start=False, stop=True),
                            reads=[b_onesb, b_bpbf])
                        b_pv[sp].w = [len(P.ops) - 1]
                        b_pv[sp].r = []
                        mk(P, "vector", lambda e, f2=f2, sp=sp, s=s: e.tensor_tensor(
                            xob[s][:, f2 * 512:(f2 + 1) * 512],
                            xst[s][:, f2 * 512:(f2 + 1) * 512], pv[sp][:], ALU.add),
                            reads=[b_xst[s], b_pv[sp]], writes=[b_xob[s]])
                    a_out_toks.append(mkdma(P, "scalar", dst[j * 128:(j + 1) * 128, :], xob[s][:], reads=[b_xob[s]]))
            return a_out_toks[-1]

        with nc.Block() as block:
            pre1 = list(ln_toks)

            def ffn1_cb(gi):
                pre1.extend(emit_ln(x_in, range(4 + 4 * gi, 8 + 4 * gi)))
                finish_is1(range(4 + 4 * gi, 8 + 4 * gi))
            if stage == 1:
                final = emit_ffn(1, x_in, out_t, pre_q1=pre1, psum_guard=pmisc_guard, mid_cb=ffn1_cb)
                st2q.close()
            elif stage == 2:
                emit_ffn(1, x_in, x1h, pre_q1=pre1, psum_guard=pmisc_guard, mid_cb=ffn1_cb)
                st2q.close()
                P.fence()
                final = [emit_attn(x1h, out_t)]
            else:
                emit_ffn(1, x_in, x1h, pre_q1=pre1, psum_guard=pmisc_guard, mid_cb=ffn1_cb)
                st2q.close()
                P.fence()
                emit_attn(x1h, x2h)
                P.fence()
                final = emit_ffn(2, x2h, out_t)
            P.finalize(nc, block, sems, final)
    return nc


def kernel(L0_attn_x, L0_attn_Wq, L0_attn_Wk, L0_attn_Wv, L0_attn_Wp, L0_attn_bp,
           L0_ffn_w_gate, L0_ffn_w_up, L0_ffn_w_down):
    x = np.ascontiguousarray(np.asarray(L0_attn_x, dtype=np.float32))
    B = x.shape[0]
    wq = np.ascontiguousarray(np.asarray(L0_attn_Wq, np.float32).reshape(C, C))
    wk = np.ascontiguousarray(np.asarray(L0_attn_Wk, np.float32).reshape(KV * D, C))
    wv = np.ascontiguousarray(np.asarray(L0_attn_Wv, np.float32).reshape(KV * D, C))
    wp = np.ascontiguousarray(np.asarray(L0_attn_Wp, np.float32))
    bp = np.ascontiguousarray(np.asarray(L0_attn_bp, np.float32).reshape(1, C))
    wg = np.ascontiguousarray(np.asarray(L0_ffn_w_gate, np.float32))
    wu = np.ascontiguousarray(np.asarray(L0_ffn_w_up, np.float32))
    wd = np.ascontiguousarray(np.asarray(L0_ffn_w_down, np.float32))
    ones = np.ones((128, 128), dtype=ml_dtypes.bfloat16)
    mask = np.triu(np.ones((128, 128))).astype(ml_dtypes.bfloat16)
    ident = np.eye(128, dtype=ml_dtypes.bfloat16)

    nc = build_program(STAGE)
    in_maps = []
    for b in range(B):
        in_maps.append({
            "x": np.ascontiguousarray(x[b]),
            "wq": wq, "wk": wk, "wv": wv, "wp": wp, "bp": bp,
            "wg": wg, "wu": wu, "wd": wd,
            "cones": ones, "cmask": mask, "cident": ident,
        })
    res = run_bass_kernel_spmd(nc, in_maps, core_ids=list(range(8)))
    out = np.stack([res.results[b]["out"] for b in range(B)], axis=0)
    return out


# version 26 (final)
# speedup vs baseline: 1.0367x; 1.0367x over previous
import contextlib
import numpy as np
import ml_dtypes
import concourse.bass as bass
import concourse.mybir as mybir
from concourse.bass_utils import run_bass_kernel_spmd

F32 = mybir.dt.float32
BF16 = mybir.dt.bfloat16
I16 = mybir.dt.int16
AX = mybir.AxisListType.X
ALU = mybir.AluOpType
AF = mybir.ActivationFunctionType

T = 2048
C = 1024
INNER = 4096
NT = 16
NG = 4
HEADS = 16
KV = 4
D = 64
EPS = 1e-5

STAGE = 3
RACES = False

ENGINES = ("tensor", "vector", "scalar", "gpsimd", "sync")


NLANE = 4


class Prog:

    def __init__(self):
        self.ops = []
        self.fence_cur = []
        self.qhist = {}

    def _add(self, engine, kind, emit, deps):
        deps = [d for d in deps if d is not None]
        if self.fence_cur:
            deps = deps + self.fence_cur
        oid = len(self.ops)
        self.ops.append({"engine": engine, "kind": kind, "emit": emit, "deps": deps})
        return oid

    def op(self, engine, emit, deps=()):
        return self._add(engine, "c", emit, list(deps))

    def dma(self, engine, out, in_, deps=(), **kw):
        def emit(eng, out=out, in_=in_, kw=kw):
            return eng.dma_start(out=out, in_=in_, **kw)
        oid = self._add(engine, "d", emit, list(deps))
        h = self.qhist.setdefault(engine, [])
        self.ops[oid]["lane"] = len(h) % NLANE
        if len(h) >= NLANE:
            self.ops[oid]["deps"].append(h[-NLANE])
        h.append(oid)
        return oid

    def qcover(self, engine):
        return list(self.qhist.get(engine, [])[-NLANE:])

    def fence(self):
        last = {}
        for i, o in enumerate(self.ops):
            if o["kind"] == "c":
                last[o["engine"]] = i
        snap = list(last.values())
        for q in self.qhist:
            snap += self.qhist[q][-NLANE:]
        self.fence_cur = snap

    def finalize(self, nc, block, sems, final_deps):
        ops = self.ops
        skey = {}
        for i, o in enumerate(ops):
            if o["kind"] == "d":
                skey[i] = (o["engine"], "d", o["lane"])
            else:
                skey[i] = (o["engine"], "c")
        needed = [o["kind"] == "d" for o in ops]
        for i, o in enumerate(ops):
            for d in o["deps"]:
                if not (skey[d] == skey[i] == ("tensor", "c")):
                    needed[d] = True
        for d in final_deps:
            needed[d] = True
        tokval = [0] * len(ops)
        counts = {}
        for i, o in enumerate(ops):
            k = skey[i]
            if needed[i]:
                counts[k] = counts.get(k, 0) + (16 if o["kind"] == "d" else 1)
                tokval[i] = counts[k]
        by_engine = {e: [] for e in ENGINES}
        for i, o in enumerate(ops):
            by_engine[o["engine"]].append(i)

        def run_for(name):
            def f(eng):
                wm = {}
                for i in by_engine[name]:
                    o = ops[i]
                    k = skey[i]
                    for d in o["deps"]:
                        dk = skey[d]
                        if dk == k and dk == ("tensor", "c"):
                            continue
                        v = tokval[d]
                        assert v > 0, f"dep {d} not inc'd"
                        if wm.get(dk, 0) < v:
                            eng.wait_ge(sems[dk], v)
                            wm[dk] = v
                    ins = o["emit"](eng)
                    if needed[i]:
                        ins.then_inc(sems[k], 16 if o["kind"] == "d" else 1)
                if name == "sync":
                    for d in final_deps:
                        dk = skey[d]
                        v = tokval[d]
                        if wm.get(dk, 0) < v:
                            eng.wait_ge(sems[dk], v)
                            wm[dk] = v
            return f

        block.sync(run_for("sync"))
        block.vector(run_for("vector"))
        block.scalar(run_for("scalar"))
        block.tensor(run_for("tensor"))
        block.gpsimd(run_for("gpsimd"))


class RW:

    def __init__(self):
        self.w = None
        self.r = []

    def rdeps(self):
        return list(self.w) if self.w else []

    def wdeps(self):
        return (list(self.w) if self.w else []) + self.r


def mk(P, engine, emit, reads=(), writes=(), extra=()):
    deps = list(extra)
    for b in reads:
        deps += b.rdeps()
    for b in writes:
        deps += b.wdeps()
    t = P.op(engine, emit, deps)
    for b in reads:
        b.r.append(t)
    for b in writes:
        b.w = [t]
        b.r = []
    return t


def mkdma(P, engine, out, in_, reads=(), writes=(), extra=(), **kw):
    deps = list(extra)
    for b in reads:
        deps += b.rdeps()
    for b in writes:
        deps += b.wdeps()
    t = P.dma(engine, out, in_, deps, **kw)
    for b in reads:
        b.r.append(t)
    for b in writes:
        b.w = [t]
        b.r = []
    return t


def build_program(stage=STAGE):
    nc = bass.Bass("TRN2", target_bir_lowering=False, detect_race_conditions=RACES)

    x_in = nc.dram_tensor("x", [T, C], F32, kind="ExternalInput")
    wq_in = nc.dram_tensor("wq", [C, C], F32, kind="ExternalInput")
    wk_in = nc.dram_tensor("wk", [KV * D, C], F32, kind="ExternalInput")
    wv_in = nc.dram_tensor("wv", [KV * D, C], F32, kind="ExternalInput")
    wp_in = nc.dram_tensor("wp", [C, C], F32, kind="ExternalInput")
    bp_in = nc.dram_tensor("bp", [1, C], F32, kind="ExternalInput")
    wg_in = nc.dram_tensor("wg", [INNER, C], F32, kind="ExternalInput")
    wu_in = nc.dram_tensor("wu", [INNER, C], F32, kind="ExternalInput")
    wd_in = nc.dram_tensor("wd", [C, INNER], F32, kind="ExternalInput")
    cones = nc.dram_tensor("cones", [128, 128], BF16, kind="ExternalInput")
    cident = nc.dram_tensor("cident", [128, 128], BF16, kind="ExternalInput")
    cmask = nc.dram_tensor("cmask", [128, 128], BF16, kind="ExternalInput")
    out_t = nc.dram_tensor("out", [T, C], F32, kind="ExternalOutput")

    tg_nat = nc.dram_tensor("tg_nat", [INNER, C], BF16)
    tu_nat = nc.dram_tensor("tu_nat", [INNER, C], BF16)
    td_nat = nc.dram_tensor("td_nat", [C, INNER], BF16)
    tgT = nc.dram_tensor("tgT", [C, INNER], BF16)
    tuT = nc.dram_tensor("tuT", [C, INNER], BF16)
    tdT = nc.dram_tensor("tdT", [INNER, C], BF16)
    q1h = nc.dram_tensor("q1h", [T, C], BF16)
    q2h = nc.dram_tensor("q2h", [T, INNER], BF16)
    x1h = nc.dram_tensor("x1h", [T, C], F32)
    x2h = nc.dram_tensor("x2h", [T, C], F32)
    ln2h = nc.dram_tensor("ln2h", [T, C], BF16)
    vh = nc.dram_tensor("vh", [KV * D, T], BF16)
    kh = nc.dram_tensor("kh", [KV * D, T], BF16)
    oh = nc.dram_tensor("oh", [T, C], BF16)
    wqbf = nc.dram_tensor("wqbf", [C, C], BF16)
    wkbf = nc.dram_tensor("wkbf", [KV * D, C], BF16)
    wvbf = nc.dram_tensor("wvbf", [KV * D, C], BF16)
    wpbf = nc.dram_tensor("wpbf", [C, C], BF16)

    P = Prog()

    with contextlib.ExitStack() as top:
        def sb(name, shape, dt=F32):
            return top.enter_context(nc.sbuf_tensor(name, shape, dt))

        sems = {}
        for e in ENGINES:
            sems[(e, "c")] = top.enter_context(nc.semaphore(f"c_{e}"))
            for l in range(NLANE):
                sems[(e, "d", l)] = top.enter_context(nc.semaphore(f"d_{e}{l}"))

        onesb = sb("onesb", [128, 128], BF16)
        maskb = sb("maskb", [128, 128], BF16)
        onesf = sb("onesf", [128, 1], F32)
        epst = sb("epst", [128, 1], F32)
        sc125 = sb("sc125", [128, 1], F32)
        bpbf = sb("bpbf", [1, C], BF16)
        b_bpbf = RW()
        meanv = {m: sb(f"meanv_{m}", [128, 1], F32) for m in "gud"}
        swv = {m: sb(f"swv_{m}", [128, 1], F32) for m in "gud"}

        b_onesb = RW(); b_maskb = RW(); b_onesf = RW(); b_epst = RW()
        b_sc125 = RW()
        b_meanv = {m: RW() for m in "gud"}
        b_swv = {m: RW() for m in "gud"}

        mkdma(P, "scalar", onesb[:], cones[:], writes=[b_onesb])
        identb = sb("identb", [128, 128], BF16)
        b_identb = RW()
        mkdma(P, "scalar", identb[:], cident[:], writes=[b_identb])
        mkdma(P, "scalar", maskb[:], cmask[:], writes=[b_maskb])
        mk(P, "vector", lambda e: e.memset(onesf[:], 1.0), writes=[b_onesf])
        mk(P, "vector", lambda e: e.memset(epst[:], EPS), writes=[b_epst])
        mk(P, "vector", lambda e: e.memset(sc125[:], 0.125), writes=[b_sc125])
        bp05 = sb("bp05", [128, 1], F32)
        bm05 = sb("bm05", [128, 1], F32)
        b_bp05 = RW(); b_bm05 = RW()
        mk(P, "vector", lambda e: e.memset(bp05[:], 0.5), writes=[b_bp05])
        mk(P, "vector", lambda e: e.memset(bm05[:], -0.5), writes=[b_bm05])
        xst = [sb(f"xst{i}", [128, C]) for i in range(2)]
        b_xst = [RW(), RW()]
        lnb = [sb(f"lnb{i}", [128, C]) for i in range(2)]
        b_lnb = [RW(), RW()]
        bnst = sb("bnst", [128, 12]); b_bnst = RW()
        mv = sb("mv", [128, 2]); b_mv = RW()
        rstd = sb("rstd", [128, 2]); b_rstd = RW()
        ast = sb("ast", [128, 4]); b_ast = RW()
        i0a = sb("i0a", [128, C], I16); b_i0a = RW()
        t1a = sb("t1a", [128, C], BF16); b_t1a = RW()
        fqa = sb("fqa", [128, C]); b_fqa = RW()
        qna1 = sb("qna", [128, C], BF16); b_qna1 = RW()
        qna = [qna1, qna1]; b_qna = [b_qna1, b_qna1]
        is1 = sb("is1", [128, NT]); b_is1 = [RW() for _ in range(NT)]
        vsv = sb("vsv", [128, NT]); b_vsv = [RW() for _ in range(NT)]
        mkdma(P, "scalar", xst[0][0:1, :], bp_in[:], writes=[b_xst[0]])
        mk(P, "scalar", lambda e: e.copy(bpbf[:], xst[0][0:1, :]), reads=[b_xst[0]], writes=[b_bpbf])

        ln_rot = [0]

        def emit_ln(src, tiles=None):
            t_q1h = []
            tl = list(tiles if tiles is not None else range(NT))
            slots = {}

            def load(i):
                sx = ln_rot[0] % 2
                ln_rot[0] += 1
                slots[i] = sx
                j = tl[i]
                mkdma(P, "gpsimd", xst[sx][:], src[j * 128:(j + 1) * 128, :],
                      writes=[b_xst[sx]])
            load(0)
            if len(tl) > 1:
                load(1)
            for i, j in enumerate(tl):
                sx = slots[i]
                s = j % 2
                mk(P, "vector", lambda e, sx=sx: e.bn_stats(bnst[:, 0:6], xst[sx][:, 0:512]),
                    reads=[b_xst[sx]], writes=[b_bnst])
                mk(P, "vector", lambda e, sx=sx: e.bn_stats(bnst[:, 6:12], xst[sx][:, 512:1024]),
                    reads=[b_xst[sx]], writes=[b_bnst])
                mk(P, "vector", lambda e: e.bn_aggr(mv[:], bnst[:]),
                    reads=[b_bnst], writes=[b_mv])
                mk(P, "vector", lambda e, j=j: e.tensor_single_scalar(
                    vsv[:, j:j + 1], mv[:, 1:2], 1.0, ALU.bypass),
                    reads=[b_mv], writes=[b_vsv[j]])
                mk(P, "vector", lambda e, s=s, sx=sx: e.tensor_single_scalar(
                    lnb[s][:], xst[sx][:], mv[:, 0:1], ALU.subtract),
                    reads=[b_xst[sx], b_mv], writes=[b_lnb[s]])
                if i + 2 < len(tl):
                    load(i + 2)
                mk(P, "vector", lambda e, s=s: e.tensor_reduce(
                    ast[:, 0:1], lnb[s][:], AX, ALU.max, apply_absolute_value=True),
                    reads=[b_lnb[s]], writes=[b_ast])
                mk(P, "vector", lambda e, j=j: e.tensor_single_scalar(
                    is1[:, j:j + 1], ast[:, 0:1], 1.0 / 127.0, ALU.mult),
                    reads=[b_ast], writes=[b_is1[j]])
                mk(P, "vector", lambda e: e.reciprocal(ast[:, 2:3], ast[:, 0:1]),
                    writes=[b_ast])
                mk(P, "vector", lambda e: e.tensor_single_scalar(
                    ast[:, 3:4], ast[:, 2:3], 127.0, ALU.mult),
                    writes=[b_ast])
                mk(P, "vector", lambda e, s=s: e.tensor_single_scalar(
                    i0a[:], lnb[s][:], ast[:, 3:4], ALU.mult),
                    reads=[b_lnb[s], b_ast], writes=[b_i0a])
                mk(P, "vector", lambda e, s=s: e.scalar_tensor_tensor(
                    fqa[:], lnb[s][:], ast[:, 3:4], i0a[:], ALU.mult, ALU.subtract),
                    reads=[b_lnb[s], b_ast, b_i0a], writes=[b_fqa])
                mk(P, "vector", lambda e: e.scalar_tensor_tensor(
                    t1a[:], fqa[:], 0.5, i0a[:], ALU.is_ge, ALU.add),
                    reads=[b_fqa, b_i0a], writes=[b_t1a])
                mk(P, "vector", lambda e, s=s: e.scalar_tensor_tensor(
                    qna[s][:], fqa[:], -0.5, t1a[:], ALU.is_le, ALU.subtract),
                    reads=[b_fqa, b_t1a], writes=[b_qna[s]])
                t_q1h.append(mkdma(P, "gpsimd", q1h[j * 128:(j + 1) * 128, :], qna[s][:], reads=[b_qna[s]]))
            return t_q1h

        def finish_is1(tiles=None):
            tt = list(tiles if tiles is not None else range(NT))
            for j in tt:
                mk(P, "scalar", lambda e, j=j: e.activation(
                    rstd[:, 0:1], vsv[:, j:j + 1], AF.Sqrt, bias=epst[:]),
                    reads=[b_vsv[j]], writes=[b_rstd], extra=b_epst.rdeps())
                mk(P, "vector", lambda e: e.reciprocal(rstd[:, 1:2], rstd[:, 0:1]),
                    writes=[b_rstd])
                mk(P, "vector", lambda e, j=j: e.tensor_tensor(
                    is1[:, j:j + 1], is1[:, j:j + 1], rstd[:, 1:2], ALU.mult),
                    reads=[b_rstd], writes=[b_is1[j]])


        st2q = top.enter_context(contextlib.ExitStack())

        def sbr(name, shape, dt=F32):
            return st2q.enter_context(nc.sbuf_tensor(name, shape, dt, side="right"))
        pmisc_rws = {m: RW() for m in "gud"}
        b_stg = [RW(), RW()]
        stg_rot = [0]
        pbuf = sbr("pbuf_q", [128, 2048], BF16)
        mbuf = sbr("mbuf_q", [128, 2048], BF16)
        tnb = mbuf
        tot = sbr("tot_q", [1, 4])
        onesrow = sbr("or_q", [1, 128])
        b_onesrow = RW(); b_pbuf = RW(); b_mbuf = RW()
        b_tnb = b_mbuf
        b_tot = {m: RW() for m in "gud"}
        mk(P, "vector", lambda e: e.memset(onesrow[:], 1.0), writes=[b_onesrow])
        partq = {m: sbr(f"part_{m}", [128, 1]) for m in "gud"}
        asumq = {m: sbr(f"asum_{m}", [128, 1]) for m in "gud"}
        stu = st2q.enter_context(contextlib.ExitStack())
        stg = [stu.enter_context(nc.psum_tensor(f"stg{i}", [128, 1024], F32))
               for i in range(2)]
        pmisc = stu.enter_context(nc.psum_tensor("pmisc", [128, 8], F32, side="right"))
        wstq = {"d": [stu.enter_context(nc.sbuf_tensor(f"wst_d{i}", [128, 1024], F32, side="right"))
                      for i in range(2)]}
        for m in "gu":
            wstq[m] = [stu.enter_context(nc.sbuf_tensor(f"wst_{m}{i}", [128, 2048], F32, side="right"))
                       for i in range(2)]
        ttW = {m: stu.enter_context(nc.sbuf_tensor(f"ttW_{m}", [128, 8, 1024], BF16, side="right"))
               for m in "gud"}
        b_wstq = {m: [RW(), RW()] for m in "gud"}
        b_partq = {m: RW() for m in "gud"}
        b_asumq = {m: RW() for m in "gud"}
        MI = {"g": 0, "u": 1, "d": 2}

        def quant_matrix(name, w_in, tT, rows, cols, route="act",
                         queue="gpsimd", wq="gpsimd", chunk_w=2048,
                         defer_p2=False):
            bw = min(cols, chunk_w)
            a = chunk_w // bw
            ncc = cols // bw
            nrb = rows // (128 * a)
            nblk = ncc * nrb
            mi = MI[name]
            wst = wstq[name]; b_wst = b_wstq[name]
            part = partq[name]; asum = asumq[name]
            b_part = b_partq[name]; b_asum = b_asumq[name]
            pmisc_rw = pmisc_rws[name]
            tw = ttW[name]

            def blocks():
                i = 0
                for hc in range(ncc):
                    for rb in range(nrb):
                        if a == 1:
                            ap = w_in[rb * 128:(rb + 1) * 128, hc * bw:(hc + 1) * bw]
                        else:
                            ap = w_in.rearrange("(b a p) c -> b p a c", a=a, p=128)[rb]
                        yield i, rb, hc, ap
                        i += 1

            mk(P, "vector", lambda e: e.memset(asum[:], 0.0), writes=[b_asum])
            for i, rb, hc, ap in blocks():
                s = i % 2
                dst = wst[s][:] if a == 1 else wst[s][:].rearrange("p (a c) -> p a c", a=a)
                mkdma(P, queue, dst, ap, writes=[b_wst[s]])
                if route == "act":
                    mk(P, "scalar", lambda e, s=s: e.activation(
                        pbuf[:, 0:chunk_w], wst[s][:], AF.Abs, accum_out=part[:]),
                        reads=[b_wst[s]], writes=[b_part, b_pbuf])
                else:
                    mk(P, "vector", lambda e, s=s: e.tensor_reduce(
                        part[:], wst[s][:], AX, ALU.add, apply_absolute_value=True),
                        reads=[b_wst[s]], writes=[b_part])
                mk(P, "vector", lambda e: e.tensor_tensor(asum[:], asum[:], part[:], ALU.add),
                    reads=[b_part], writes=[b_asum])
            c0, c1 = 2 * mi, 2 * mi + 1
            mk(P, "tensor", lambda e: e.matmul(
                pmisc[0:1, c0:c0 + 1], asum[:], onesf[:], start=True, stop=True),
                reads=[b_asum, b_onesf], writes=[pmisc_rw])
            mk(P, "vector", lambda e: e.tensor_scalar(
                tot[0:1, mi:mi + 1], pmisc[0:1, c0:c0 + 1], 1.0 / (rows * cols), 1e-5, ALU.mult, ALU.max),
                reads=[pmisc_rw], writes=[b_tot[name]])
            mk(P, "tensor", lambda e: e.matmul(
                pmisc[:, c1:c1 + 1], onesrow[:], tot[0:1, mi:mi + 1], start=True, stop=True),
                reads=[b_onesrow, b_tot[name]], writes=[pmisc_rw])
            mk(P, "vector", lambda e: e.reciprocal(swv[name][:], pmisc[:, c1:c1 + 1]),
                reads=[pmisc_rw], writes=[b_swv[name]])
            msc = 0.5 if route == "act" else 1.0
            mk(P, "vector", lambda e, msc=msc: e.tensor_single_scalar(
                meanv[name][:], pmisc[:, c1:c1 + 1], msc, ALU.mult),
                reads=[pmisc_rw], writes=[b_meanv[name]])

            def pass2():
                cnt = 0
                ev_tok = {0: [], 1: []}
                fl_tok = {0: [], 1: []}
                pend = []
                blist = list(blocks())
                evac_eng = "scalar"

                def emit_pend():
                    for fn in pend:
                        fn()
                    del pend[:]

                for bi, (i, rb, hc, ap) in enumerate(blist):
                    s = i % 2
                    dst = wst[s][:] if a == 1 else wst[s][:].rearrange("p (a c) -> p a c", a=a)
                    mkdma(P, queue, dst, ap, writes=[b_wst[s]])
                    emit_pend()
                    if route == "act":
                        mk(P, "scalar", lambda e, s=s: e.activation(
                            pbuf[:, 0:chunk_w], wst[s][:], AF.Sign, scale=swv[name][:], bias=bm05[:]),
                            reads=[b_wst[s], b_swv[name]], writes=[b_pbuf], extra=b_bm05.rdeps())
                        mk(P, "scalar", lambda e, s=s: e.activation(
                            mbuf[:, 0:chunk_w], wst[s][:], AF.Sign, scale=swv[name][:], bias=bp05[:]),
                            reads=[b_wst[s], b_swv[name]], writes=[b_mbuf], extra=b_bp05.rdeps())
                        mk(P, "vector", lambda e: e.scalar_tensor_tensor(
                            tnb[:, 0:chunk_w], pbuf[:, 0:chunk_w], -1.0, mbuf[:, 0:chunk_w],
                            ALU.mult, ALU.subtract),
                            reads=[b_pbuf], writes=[b_tnb])
                    else:
                        mk(P, "vector", lambda e, s=s: e.tensor_scalar(
                            pbuf[:, 0:chunk_w], wst[s][:], swv[name][:], 0.5, ALU.mult, ALU.is_ge),
                            reads=[b_wst[s], b_swv[name]], writes=[b_pbuf])
                        mk(P, "vector", lambda e, s=s: e.tensor_scalar(
                            mbuf[:, 0:chunk_w], wst[s][:], swv[name][:], -0.5, ALU.mult, ALU.is_le),
                            reads=[b_wst[s], b_swv[name]], writes=[b_mbuf])
                        mk(P, "vector", lambda e: e.tensor_tensor(
                            tnb[:, 0:chunk_w], mbuf[:, 0:chunk_w], pbuf[:, 0:chunk_w], ALU.subtract),
                            reads=[b_pbuf], writes=[b_tnb])
                    for a_ in range(a):
                        par = stg_rot[0] % 2
                        stg_rot[0] += 1
                        war = b_stg[par].wdeps()
                        for cb in range(8):
                            mk(P, "tensor", lambda e, par=par, a_=a_, cb=cb: e.matmul(
                                stg[par][:, cb * 128:(cb + 1) * 128],
                                tnb[:, a_ * bw + cb * 128:a_ * bw + (cb + 1) * 128],
                                identb[:], start=True, stop=True),
                                reads=[b_tnb], extra=(war + b_identb.rdeps() if cb == 0 else []))
                        b_stg[par].w = [len(P.ops) - 1]
                        b_stg[par].r = []
                        half = (cnt // 4) % 2
                        roff = half * 512 + (cnt % 4) * 128
                        flush = (cnt % 4 == 3)
                        rb_, hc_, i_, cnt_ = rb, hc, i, cnt

                        def do_ev(par=par, roff=roff, half=half, flush=flush,
                                  rb=rb_, hc=hc_, i=i_, a_=a_, cnt=cnt_):
                            if evac_eng == "scalar":
                                ev = mk(P, evac_eng, lambda e: e.copy(
                                    tw[:, :, roff:roff + 128],
                                    stg[par][:].rearrange("p (cb r) -> p cb r", cb=8)),
                                    reads=[b_stg[par]], extra=fl_tok[half])
                            else:
                                ev = mk(P, evac_eng, lambda e: e.tensor_copy(
                                    tw[:, :, roff:roff + 128],
                                    stg[par][:].rearrange("p (cb r) -> p cb r", cb=8)),
                                    reads=[b_stg[par]], extra=fl_tok[half])
                            ev_tok[half].append(ev)
                            if flush:
                                fl_tok[half] = []
                                for cb in range(8):
                                    if a == 1:
                                        pblk = hc * 8 + cb
                                        dstT = tT[pblk * 128:(pblk + 1) * 128,
                                                  (rb - 3) * 128:(rb + 1) * 128]
                                    else:
                                        rs = (cnt - 3) * 128
                                        dstT = tT[cb * 128:(cb + 1) * 128, rs:rs + 512]
                                    fl_tok[half].append(mkdma(
                                        P, wq, dstT, tw[:, cb, half * 512:half * 512 + 512],
                                        extra=ev_tok[half]))
                                ev_tok[half] = []
                        pend.append(do_ev)
                        cnt += 1
                emit_pend()
                return P.qcover(wq)
            if defer_p2:
                return pass2
            return pass2()

        def mid1():
            mst = contextlib.ExitStack()
            wstg = mst.enter_context(nc.sbuf_tensor("awst", [128, 512], F32, side="right"))
            wbfg = mst.enter_context(nc.sbuf_tensor("awbf", [128, 512], BF16, side="right"))
            b_wstg = RW(); b_wbfg = RW()
            for w_in_, wbf_, nrow in ((wq_in, wqbf, C), (wk_in, wkbf, KV * D),
                                      (wv_in, wvbf, KV * D), (wp_in, wpbf, C)):
                for t in range(nrow // 128):
                    for hf in range(2):
                        mkdma(P, "gpsimd", wstg[:], w_in_[t * 128:(t + 1) * 128, hf * 512:(hf + 1) * 512],
                              writes=[b_wstg])
                        mk(P, "gpsimd", lambda e: e.tensor_copy(wbfg[:], wstg[:]),
                            reads=[b_wstg], writes=[b_wbfg])
                        mkdma(P, "scalar", wbf_[t * 128:(t + 1) * 128, hf * 512:(hf + 1) * 512],
                              wbfg[:], reads=[b_wbfg])
            mst.close()

        d_p2 = quant_matrix("d", wd_in, tdT, C, INNER, queue="sync",
                            wq="scalar", chunk_w=1024, defer_p2=True)
        t_tgT = quant_matrix("g", wg_in, tgT, INNER, C, queue="gpsimd", wq="gpsimd")
        ln_toks = emit_ln(x_in, range(0, 4))
        finish_is1(range(0, 4))
        t_tuT = quant_matrix("u", wu_in, tuT, INNER, C, queue="sync", wq="sync")
        t_tdT = d_p2()
        ln_toks += emit_ln(x_in, range(4, 16))
        finish_is1(range(4, 16))
        mid1()
        stu.close()
        pmisc_guard = []
        for m in "gud":
            pmisc_guard += pmisc_rws[m].rdeps() + list(pmisc_rws[m].r)
        for bs in b_stg:
            pmisc_guard += bs.rdeps() + list(bs.r)
        for q in ("scalar", "gpsimd", "sync"):
            pmisc_guard += P.qcover(q)
        tern_done = {"g": t_tgT, "u": t_tuT, "d": t_tdT}

        def emit_ln2(src2, tiles, dep):
            toks = []
            for j in tiles:
                s = j % 2
                mkdma(P, "scalar", xst[s][:], src2[j * 128:(j + 1) * 128, :],
                      writes=[b_xst[s]], extra=dep)
                mk(P, "vector", lambda e, s=s: e.bn_stats(bnst[:, 0:6], xst[s][:, 0:512]),
                    reads=[b_xst[s]], writes=[b_bnst])
                mk(P, "vector", lambda e, s=s: e.bn_stats(bnst[:, 6:12], xst[s][:, 512:1024]),
                    reads=[b_xst[s]], writes=[b_bnst])
                mk(P, "vector", lambda e: e.bn_aggr(mv[:], bnst[:]),
                    reads=[b_bnst], writes=[b_mv])
                mk(P, "scalar", lambda e: e.activation(rstd[:, 0:1], mv[:, 1:2], AF.Sqrt, bias=epst[:]),
                    reads=[b_mv], writes=[b_rstd], extra=b_epst.rdeps())
                mk(P, "vector", lambda e: e.reciprocal(rstd[:, 1:2], rstd[:, 0:1]),
                    writes=[b_rstd])
                mk(P, "vector", lambda e, s=s: e.tensor_scalar(
                    t1a[:], xst[s][:], mv[:, 0:1], rstd[:, 1:2], ALU.subtract, ALU.mult),
                    reads=[b_xst[s], b_mv, b_rstd], writes=[b_t1a])
                toks.append(mkdma(P, "scalar", ln2h[j * 128:(j + 1) * 128, :], t1a[:],
                                  reads=[b_t1a]))
            return toks

        def emit_ffn(call, src, dst, pre_q1=None, mid_cb=None, psum_guard=(),
                     post2_cb=None):
            out_toks = []
            with contextlib.ExitStack() as st2:
                def sb2(name, shape, dt=F32):
                    return st2.enter_context(nc.sbuf_tensor(f"{name}_{call}", shape, dt))
                sgv = sb2("sgv", [128, NT])
                cvv = sb2("cvv", [128, NT])
                b_sgv = [RW() for _ in range(NT)]
                b_cvv = [RW() for _ in range(NT)]
                sepv = sb2("sepv", [128, NT])
                b_sepv = [RW() for _ in range(NT)]
                csv = sb2("csv", [128, NT])
                b_csv = [RW() for _ in range(NT)]
                q1t = sb2("q1t", [128, 8, 512], BF16)
                b_q1t = RW()
                wsl = [sb2(f"wsl{i}", [128, 8, 512], BF16) for i in range(3)]
                b_wsl = [RW(), RW(), RW()]
                sg1 = sb2("sg1", [128, 512])
                sg = [sg1, sg1]
                b_sg1 = RW()
                b_sg = [b_sg1, b_sg1]
                hb = sb2("hb", [128, 4, INNER])
                b_hb = [[RW() for _ in range(8)] for _ in range(4)]
                i0h = sb2("i0h", [128, INNER], I16)
                fqh = sb2("fqh", [128, INNER], BF16)
                t1h = sb2("t1h", [128, INNER], BF16)
                b_i0h = RW(); b_fqh = RW(); b_t1h = RW()
                q2t = sb2("q2t", [128, 8, 512], BF16)
                b_q2t = [RW() for _ in range(8)]
                wds = sb2("wds", [128, 2, 4, 1024], BF16)
                b_wds = [RW(), RW()]
                xout = sb2("xout", [128, 2, C])
                b_xout = [RW(), RW()]
                pd = [st2.enter_context(nc.psum_tensor(f"pd{i}_{call}", [128, 512], F32)) for i in range(4)]
                pg = [st2.enter_context(nc.psum_tensor(f"pg{i}_{call}", [128, 512], F32)) for i in range(2)]
                pu = [st2.enter_context(nc.psum_tensor(f"pu{i}_{call}", [128, 512], F32)) for i in range(2)]
                b_pg = [RW(), RW()]; b_pu = [RW(), RW()]; b_pd = [RW(), RW(), RW(), RW()]
                for rwb in (b_pg + b_pu + b_pd + b_sgv + b_cvv + b_sepv + b_csv
                            + [b_q1t, b_sg1, b_i0h, b_fqh, b_t1h] + b_wsl
                            + [x for row in b_hb for x in row]
                            + b_q2t + list(b_wds) + list(b_xout)):
                    rwb.r += list(psum_guard)

                if pre_q1 is None:
                    t_q1h = emit_ln(src)
                    finish_is1()
                else:
                    t_q1h = pre_q1

                for j in range(NT):
                    mk(P, "vector", lambda e, j=j: e.tensor_tensor(
                        sgv[:, j:j + 1], is1[:, j:j + 1], meanv["g"][:], ALU.mult),
                        reads=[b_is1[j], b_meanv["g"]], writes=[b_sgv[j]])
                    mk(P, "vector", lambda e, j=j: e.tensor_tensor(
                        cvv[:, j:j + 1], is1[:, j:j + 1], meanv["u"][:], ALU.mult),
                        reads=[b_is1[j], b_meanv["u"]], writes=[b_cvv[j]])

                tgT_r = tgT.rearrange("(c p) i -> p c i", p=128)
                tuT_r = tuT.rearrange("(c p) i -> p c i", p=128)
                tdT_r = tdT.rearrange("(k p) c -> p k c", p=128)
                q1t_tok = None

                t_q2_done = {}

                def phase1(g):
                    q2toks = []
                    gs, ge = g * 512, (g + 1) * 512
                    q1t_war = b_q1t.wdeps()
                    for f in range(8):
                        mkdma(P, "sync", q1t[:, f, :],
                              q1h[gs:ge, f * 128:(f + 1) * 128],
                              extra=q1t_war + t_q1h[g * 4:(g + 1) * 4], transpose=True)
                    b_q1t.w = P.qcover("sync")
                    b_q1t.r = []

                    for n in range(8):
                        sg_slot = (2 * n) % 3
                        su_slot = (2 * n + 1) % 3
                        mkdma(P, "gpsimd", wsl[sg_slot][:], tgT_r[:, :, n * 512:(n + 1) * 512],
                              writes=[b_wsl[sg_slot]], extra=tern_done["g"])
                        mkdma(P, "gpsimd", wsl[su_slot][:], tuT_r[:, :, n * 512:(n + 1) * 512],
                              writes=[b_wsl[su_slot]], extra=tern_done["u"])
                        for j4 in range(4):
                            j = g * 4 + j4
                            sp = j4 % 2
                            for f in range(8):
                                mk(P, "tensor", lambda e, f=f, sp=sp, j4=j4, s=sg_slot: e.matmul(
                                    pg[sp][:], q1t[:, f, j4 * 128:(j4 + 1) * 128],
                                    wsl[s][:, f, :], start=(f == 0), stop=(f == 7)),
                                    reads=[b_q1t, b_wsl[sg_slot]],
                                    writes=[b_pg[sp]] if f == 0 else [],
                                    extra=([] if f else b_pg[sp].wdeps()))
                                if f == 7:
                                    b_pg[sp].w = [len(P.ops) - 1]
                                    b_pg[sp].r = []
                            mk(P, "scalar", lambda e, sp=sp, j=j: e.activation(
                                sg[sp][:], pg[sp][:], AF.Silu, scale=sgv[:, j:j + 1]),
                                reads=[b_pg[sp], b_sgv[j]], writes=[b_sg[sp]])
                            for f in range(8):
                                mk(P, "tensor", lambda e, f=f, sp=sp, j4=j4, s=su_slot: e.matmul(
                                    pu[sp][:], q1t[:, f, j4 * 128:(j4 + 1) * 128],
                                    wsl[s][:, f, :], start=(f == 0), stop=(f == 7)),
                                    reads=[b_q1t, b_wsl[su_slot]],
                                    writes=[b_pu[sp]] if f == 0 else [],
                                    extra=([] if f else b_pu[sp].wdeps()))
                                if f == 7:
                                    b_pu[sp].w = [len(P.ops) - 1]
                                    b_pu[sp].r = []
                            mk(P, "vector", lambda e, sp=sp, j4=j4, n=n: e.tensor_tensor(
                                hb[:, j4, n * 512:(n + 1) * 512], sg[sp][:], pu[sp][:], ALU.mult),
                                reads=[b_sg[sp], b_pu[sp]], writes=[b_hb[j4][n]])

                    for j4 in range(4):
                        j = g * 4 + j4
                        s = j4 % 2
                        hrw = b_hb[j4]
                        hdeps = [t for x in hrw for t in (x.w or [])]
                        mk(P, "vector", lambda e, j4=j4: e.tensor_reduce(
                            ast[:, 0:1], hb[:, j4, :], AX, ALU.max, apply_absolute_value=True),
                            writes=[b_ast], extra=hdeps)
                        mk(P, "vector", lambda e, j=j: e.tensor_tensor(
                            ast[:, 1:2], ast[:, 0:1], cvv[:, j:j + 1], ALU.mult),
                            reads=[b_cvv[j]], writes=[b_ast])
                        mk(P, "vector", lambda e: e.tensor_single_scalar(
                            ast[:, 1:2], ast[:, 1:2], 1e-5, ALU.max),
                            writes=[b_ast])
                        mk(P, "vector", lambda e: e.reciprocal(ast[:, 2:3], ast[:, 1:2]),
                            writes=[b_ast])
                        mk(P, "vector", lambda e, j=j: e.tensor_single_scalar(
                            ast[:, 3:4], ast[:, 2:3], 127.0, ALU.mult),
                            writes=[b_ast])
                        mk(P, "vector", lambda e, j=j: e.tensor_tensor(
                            csv[:, j:j + 1], ast[:, 3:4], cvv[:, j:j + 1], ALU.mult),
                            reads=[b_ast, b_cvv[j]], writes=[b_csv[j]])
                        mk(P, "vector", lambda e, j=j: e.tensor_single_scalar(
                            ast[:, 1:2], ast[:, 1:2], 0.5 / 127.0, ALU.mult),
                            writes=[b_ast])
                        mk(P, "vector", lambda e, j=j: e.tensor_tensor(
                            sepv[:, j:j + 1], ast[:, 1:2], meanv["d"][:], ALU.mult),
                            reads=[b_ast, b_meanv["d"]], writes=[b_sepv[j]])
                        mk(P, "vector", lambda e, j4=j4, j=j: e.tensor_single_scalar(
                            i0h[:], hb[:, j4, :], csv[:, j:j + 1], ALU.mult),
                            reads=[b_csv[j]], writes=[b_i0h], extra=hdeps)
                        mk(P, "vector", lambda e, j4=j4, j=j: e.scalar_tensor_tensor(
                            fqh[:], hb[:, j4, :], csv[:, j:j + 1], i0h[:], ALU.mult, ALU.subtract),
                            reads=[b_csv[j], b_i0h], writes=[b_fqh], extra=hdeps)
                        mk(P, "vector", lambda e: e.scalar_tensor_tensor(
                            t1h[:], fqh[:], 0.5, i0h[:], ALU.is_ge, ALU.add),
                            reads=[b_fqh, b_i0h], writes=[b_t1h])
                        if call == 2:
                            mk(P, "vector", lambda e: e.scalar_tensor_tensor(
                                t1h[:], fqh[:], 1.5, t1h[:], ALU.is_ge, ALU.add),
                                reads=[b_fqh], writes=[b_t1h])
                            mk(P, "vector", lambda e: e.scalar_tensor_tensor(
                                t1h[:], fqh[:], -0.5, t1h[:], ALU.is_le, ALU.subtract),
                                reads=[b_fqh], writes=[b_t1h])
                            hfree = mk(P, "vector", lambda e: e.scalar_tensor_tensor(
                                fqh[:], fqh[:], -1.5, t1h[:], ALU.is_le, ALU.add),
                                reads=[b_t1h], writes=[b_fqh])
                        else:
                            hfree = mk(P, "vector", lambda e: e.scalar_tensor_tensor(
                                fqh[:], fqh[:], -0.5, t1h[:], ALU.is_le, ALU.subtract),
                                reads=[b_t1h], writes=[b_fqh])
                        for x in hrw:
                            x.r.append(hfree)
                        q2toks.append(mkdma(P, "scalar", q2h[j * 128:(j + 1) * 128, :], fqh[:], reads=[b_fqh]))
                    t_q2_done[g] = list(q2toks)

                def phase2(g):
                    gs, ge = g * 512, (g + 1) * 512
                    q2cover = t_q2_done[g]
                    dbanks = [pd[0], pd[1], pd[2], pd[3], pg[0], pg[1], pu[0], pu[1]]
                    dbrw = [b_pd[0], b_pd[1], b_pd[2], b_pd[3], b_pg[0], b_pg[1], b_pu[0], b_pu[1]]
                    for ha in range(8):
                        ws = ha % 2
                        for k4 in range(4):
                            w = ha * 4 + k4
                            mkdma(P, "sync",
                                  q2t[:, w % 8, :], q2h[gs:ge, w * 128:(w + 1) * 128],
                                  writes=[b_q2t[w % 8]], extra=q2cover, transpose=True)
                        mkdma(P, "gpsimd", wds[:, ws, :, :], tdT_r[:, ha * 4:(ha + 1) * 4, :],
                              writes=[b_wds[ws]], extra=tern_done["d"])
                        for j4 in range(4):
                            for f2 in range(2):
                                pdi = j4 * 2 + f2
                                for k4 in range(4):
                                    w = ha * 4 + k4
                                    mk(P, "tensor", lambda e, w=w, k4=k4, ws=ws, j4=j4, f2=f2, pdi=pdi, ha=ha: e.matmul(
                                        dbanks[pdi][:], q2t[:, w % 8, j4 * 128:(j4 + 1) * 128],
                                        wds[:, ws, k4, f2 * 512:(f2 + 1) * 512],
                                        start=(ha == 0 and k4 == 0), stop=(ha == 7 and k4 == 3)),
                                        reads=[b_q2t[w % 8], b_wds[ws]],
                                        writes=[dbrw[pdi]] if (ha == 0 and k4 == 0) else [],
                                        extra=([] if (ha or k4) else dbrw[pdi].wdeps()))
                                    if ha == 7 and k4 == 3:
                                        dbrw[pdi].w = [len(P.ops) - 1]
                                        dbrw[pdi].r = []
                    for j4 in range(4):
                        j = g * 4 + j4
                        s = j4 % 2
                        mkdma(P, "gpsimd", xst[s][:], src[j * 128:(j + 1) * 128, :], writes=[b_xst[s]])
                        for f2 in range(2):
                            pdi = j4 * 2 + f2
                            mk(P, "vector", lambda e, j4=j4, f2=f2, j=j, s=s, pdi=pdi: e.scalar_tensor_tensor(
                                xout[:, j4 % 2, f2 * 512:(f2 + 1) * 512],
                                dbanks[pdi][:],
                                sepv[:, j:j + 1],
                                xst[s][:, f2 * 512:(f2 + 1) * 512], ALU.mult, ALU.add),
                                reads=[dbrw[pdi], b_sepv[j], b_xst[s]], writes=[b_xout[j4 % 2]])
                        out_toks.append(mkdma(P, "scalar", dst[j * 128:(j + 1) * 128, :],
                                              xout[:, j4 % 2, :], reads=[b_xout[j4 % 2]]))

                phase1(0)
                if mid_cb:
                    mid_cb(0)
                phase1(1)
                if mid_cb:
                    mid_cb(1)
                phase1(2)
                if mid_cb:
                    mid_cb(2)
                for g in range(NG):
                    if g + 3 < NG:
                        phase1(g + 3)
                    phase2(g)
                    if post2_cb:
                        post2_cb(g, out_toks[g * 4:(g + 1) * 4])
            return out_toks

        def emit_attn(src, dst, pre_ln2=None):
            a_out_toks = []
            with contextlib.ExitStack() as st2:
                def sb2(name, shape, dt=F32):
                    return st2.enter_context(nc.sbuf_tensor(name, shape, dt))
                lnbf1 = sb2("alnbf", [128, C], BF16)
                lnbf = [lnbf1, lnbf1]
                b_lnbf1 = RW()
                b_lnbf = [b_lnbf1, b_lnbf1]
                wpT = sb2("wpT", [128, 8, C], BF16); b_wpT = RW()
                rz = sb2("rz", [128, 1]); b_rz = RW()
                xob1 = sb2("axob", [128, C])
                xob = [xob1, xob1]
                b_xob1 = RW()
                b_xob = [b_xob1, b_xob1]
                stA = st2.enter_context(contextlib.ExitStack())
                def sbA(name, shape, dt=F32):
                    return stA.enter_context(nc.sbuf_tensor(name, shape, dt))
                ln2t = [sbA(f"ln2t{i}", [128, 8, 512], BF16) for i in range(2)]
                b_ln2t = [RW(), RW()]
                wqT = sbA("wqT", [128, 8, C], BF16); b_wqT = RW()
                qfm = sbA("qfm", [128, 8, T], BF16); b_qfm = RW()
                kfm2 = sbA("kfm2", [128, 4, T], BF16); b_kfm = RW()
                vstg = lnbf1[:, 0:512]; b_vstg = b_lnbf1
                vt65 = sbA("vt65", [128, 16, 260], BF16); b_vt65 = RW()
                strips = sbA("strips", [128, 16, 512], BF16)
                b_str = [RW() for _ in range(16)]
                otm = sbA("otm", [128, 16, C], BF16)
                b_otm = RW()
                stB = stA.enter_context(contextlib.ExitStack())
                wkT = stB.enter_context(nc.sbuf_tensor("wkT", [128, 8, KV * D], BF16)); b_wkT = RW()
                wvT = stB.enter_context(nc.sbuf_tensor("wvT", [128, 8, KV * D], BF16)); b_wvT = RW()

                pp = [st2.enter_context(nc.psum_tensor(f"app{i}", [128, 1024], F32)) for i in range(2)]
                b_pp = [RW(), RW()]
                pv = [st2.enter_context(nc.psum_tensor(f"apv{i}", [128, 512], F32)) for i in range(4)]
                b_pv = [RW(), RW(), RW(), RW()]

                if pre_ln2 is None:
                    for j in range(NT):
                        s = j % 2
                        mkdma(P, "scalar", xst[s][:], src[j * 128:(j + 1) * 128, :], writes=[b_xst[s]])
                        mk(P, "vector", lambda e, s=s: e.bn_stats(bnst[:, 0:6], xst[s][:, 0:512]),
                            reads=[b_xst[s]], writes=[b_bnst])
                        mk(P, "vector", lambda e, s=s: e.bn_stats(bnst[:, 6:12], xst[s][:, 512:1024]),
                            reads=[b_xst[s]], writes=[b_bnst])
                        mk(P, "vector", lambda e: e.bn_aggr(mv[:], bnst[:]),
                            reads=[b_bnst], writes=[b_mv])
                        mk(P, "scalar", lambda e: e.activation(rstd[:, 0:1], mv[:, 1:2], AF.Sqrt, bias=epst[:]),
                            reads=[b_mv], writes=[b_rstd], extra=b_epst.rdeps())
                        mk(P, "vector", lambda e: e.reciprocal(rstd[:, 1:2], rstd[:, 0:1]),
                            writes=[b_rstd])
                        mk(P, "vector", lambda e, s=s: e.tensor_scalar(
                            lnbf[s][:], xst[s][:], mv[:, 0:1], rstd[:, 1:2], ALU.subtract, ALU.mult),
                            reads=[b_xst[s], b_mv, b_rstd], writes=[b_lnbf[s]])
                        mkdma(P, "scalar", ln2h[j * 128:(j + 1) * 128, :], lnbf[s][:], reads=[b_lnbf[s]])
                    ln2_cover = P.qcover("scalar")
                else:
                    ln2_cover = list(pre_ln2)

                for buf, brw, hbmsrc, nr in ((wqT, b_wqT, wqbf, C), (wkT, b_wkT, wkbf, KV * D),
                                             (wvT, b_wvT, wvbf, KV * D), (wpT, b_wpT, wpbf, C)):
                    for f in range(8):
                        mkdma(P, "sync", buf[:, f, 0:nr], hbmsrc[:, f * 128:(f + 1) * 128],
                              transpose=True)
                    brw.w = P.qcover("sync")
                    brw.r = []

                for tb in range(4):
                    lb = tb % 2
                    ln2t_war = b_ln2t[lb].wdeps()
                    for f in range(8):
                        mkdma(P, "sync", ln2t[lb][:, f, :],
                              ln2h[tb * 512:(tb + 1) * 512, f * 128:(f + 1) * 128],
                              extra=ln2t_war + ln2_cover, transpose=True)
                    b_ln2t[lb].w = P.qcover("sync")
                    b_ln2t[lb].r = []
                    kk = 0
                    for qdst, wT, b_wT, nm in ((qfm, wqT, b_wqT, 8), (kh, wkT, b_wkT, 2), (vh, wvT, b_wvT, 2)):
                        for m in range(nm):
                            sp = kk % 2
                            kk += 1
                            for f in range(8):
                                mk(P, "tensor", lambda e, m=m, f=f, sp=sp, wT=wT, lb=lb: e.matmul(
                                    pp[sp][:, 0:512], wT[:, f, m * 128:(m + 1) * 128],
                                    ln2t[lb][:, f, :],
                                    start=(f == 0), stop=(f == 7)),
                                    reads=[b_wT, b_ln2t[lb]],
                                    writes=[b_pp[sp]] if f == 0 else [],
                                    extra=([] if f else b_pp[sp].wdeps()))
                                if f == 7:
                                    b_pp[sp].w = [len(P.ops) - 1]
                                    b_pp[sp].r = []
                            if qdst is qfm:
                                mk(P, "vector", lambda e, m=m, tb=tb, sp=sp: e.tensor_copy(
                                    qfm[:, m, tb * 512:(tb + 1) * 512], pp[sp][:, 0:512]),
                                    reads=[b_pp[sp]], writes=[b_qfm])
                            else:
                                mk(P, "vector", lambda e, sp=sp: e.tensor_copy(vstg, pp[sp][:, 0:512]),
                                    reads=[b_pp[sp]], writes=[b_vstg])
                                mkdma(P, "scalar", qdst[m * 128:(m + 1) * 128, tb * 512:(tb + 1) * 512],
                                      vstg, reads=[b_vstg])
                kv_cover = P.qcover("scalar")
                kfm_war = b_kfm.wdeps()
                for g2 in range(KV):
                    for off in range(2):
                        mkdma(P, "gpsimd", kfm2[off * 64:(off + 1) * 64, g2, :],
                              kh[g2 * 64:(g2 + 1) * 64, :], extra=kfm_war + kv_cover)
                b_kfm.w = P.qcover("gpsimd")
                b_kfm.r = []

                stB.close()
                with nc.sbuf_tensor("vtm4", [128, 16, 256], BF16) as vtm4:
                    b_vtm4 = RW()
                    for t in range(16):
                        mkdma(P, "sync", vtm4[:, t, :], vh[:, t * 128:(t + 1) * 128],
                              extra=kv_cover, transpose=True)
                    b_vtm4.w = P.qcover("sync")
                    b_vtm4.r = []
                    mk(P, "vector", lambda e: e.memset(vt65[:], 1.0), writes=[b_vt65])
                    for t in range(16):
                        for g2 in range(4):
                            mk(P, "vector", lambda e, t=t, g2=g2: e.tensor_copy(
                                vt65[:, t, g2 * 65:g2 * 65 + 64], vtm4[:, t, g2 * 64:(g2 + 1) * 64]),
                                reads=[b_vtm4], writes=[b_vt65])

                for h in range(HEADS):
                    g2 = h // 4
                    qch, qoff = h // 2, 64 * (h % 2)
                    for J in range(4):
                        ntile = 4 * J + 4
                        for pi in range(ntile // 2):
                            sp = pi % 2
                            for half in range(2):
                                t = pi * 2 + half
                                mk(P, "tensor", lambda e, t=t, half=half, sp=sp, J=J, g2=g2, qch=qch, qoff=qoff: e.matmul(
                                    pp[sp][:, half * 512:(half + 1) * 512],
                                    kfm2[qoff:qoff + 64, g2, t * 128:(t + 1) * 128],
                                    qfm[qoff:qoff + 64, qch, J * 512:(J + 1) * 512],
                                    start=True, stop=True),
                                    reads=[b_kfm, b_qfm],
                                    writes=[b_pp[sp]] if half == 0 else [],
                                    extra=([] if half else b_pp[sp].wdeps()))
                                if half == 1:
                                    b_pp[sp].w = [len(P.ops) - 1]
                                    b_pp[sp].r = []
                            t0 = pi * 2
                            texp = mk(P, "scalar", lambda e, t0=t0, sp=sp: e.activation(
                                strips[:, t0:t0 + 2, :], pp[sp][:], AF.Exp, scale=sc125[:]),
                                reads=[b_pp[sp], b_sc125],
                                writes=[b_str[t0], b_str[t0 + 1]])
                        for t in range(4 * J, 4 * J + 4):
                            p = t - 4 * J
                            mk(P, "vector", lambda e, t=t, p=p: e.tensor_tensor(
                                strips[:, t, p * 128:(p + 1) * 128],
                                strips[:, t, p * 128:(p + 1) * 128], maskb[:], ALU.mult),
                                reads=[b_maskb], writes=[b_str[t]])
                        for s4 in range(4):
                            jj = J * 4 + s4
                            sp = s4
                            nk = 4 * J + s4 + 1
                            for t in range(nk):
                                mk(P, "tensor", lambda e, t=t, s4=s4, sp=sp, g2=g2, nk=nk: e.matmul(
                                    pv[sp][:, 0:65], strips[:, t, s4 * 128:(s4 + 1) * 128],
                                    vt65[:, t, g2 * 65:(g2 + 1) * 65],
                                    start=(t == 0), stop=(t == nk - 1)),
                                    reads=[b_str[t], b_vt65],
                                    writes=[b_pv[sp]] if t == 0 else [],
                                    extra=([] if t else b_pv[sp].wdeps()))
                                if t == nk - 1:
                                    b_pv[sp].w = [len(P.ops) - 1]
                                    b_pv[sp].r = []
                            mk(P, "vector", lambda e, sp=sp: e.reciprocal(rz[:], pv[sp][:, 64:65]),
                                reads=[b_pv[sp]], writes=[b_rz])
                            mk(P, "vector", lambda e, jj=jj, h=h, sp=sp: e.tensor_single_scalar(
                                otm[:, jj, h * 64:(h + 1) * 64], pv[sp][:, 0:64], rz[:],
                                ALU.mult),
                                reads=[b_pv[sp], b_rz], writes=[b_otm])

                for j in range(NT):
                    mkdma(P, "scalar", oh[j * 128:(j + 1) * 128, :], otm[:, j, :], reads=[b_otm])
                oh_cover = P.qcover("scalar")
                stA.close()
                oT = sb2("oT", [128, 8, T], BF16)
                b_oT = RW()
                for f in range(8):
                    mkdma(P, "sync", oT[:, f, :], oh[:, f * 128:(f + 1) * 128],
                          extra=oh_cover, transpose=True)
                b_oT.w = P.qcover("sync")
                b_oT.r = []
                for j in range(NT):
                    s = j % 2
                    mkdma(P, "scalar", xst[s][:], src[j * 128:(j + 1) * 128, :], writes=[b_xst[s]])
                    for f2 in range(2):
                        sp = f2
                        for f in range(8):
                            mk(P, "tensor", lambda e, j=j, f=f, f2=f2, sp=sp: e.matmul(
                                pv[sp][:], oT[:, f, j * 128:(j + 1) * 128],
                                wpT[:, f, f2 * 512:(f2 + 1) * 512],
                                start=(f == 0), stop=False),
                                reads=[b_oT, b_wpT],
                                writes=[b_pv[sp]] if f == 0 else [],
                                extra=([] if f else b_pv[sp].wdeps()))
                        mk(P, "tensor", lambda e, f2=f2, sp=sp: e.matmul(
                            pv[sp][:], onesb[0:1, :], bpbf[0:1, f2 * 512:(f2 + 1) * 512],
                            start=False, stop=True),
                            reads=[b_onesb, b_bpbf])
                        b_pv[sp].w = [len(P.ops) - 1]
                        b_pv[sp].r = []
                        mk(P, "vector", lambda e, f2=f2, sp=sp, s=s: e.tensor_tensor(
                            xob[s][:, f2 * 512:(f2 + 1) * 512],
                            xst[s][:, f2 * 512:(f2 + 1) * 512], pv[sp][:], ALU.add),
                            reads=[b_xst[s], b_pv[sp]], writes=[b_xob[s]])
                    a_out_toks.append(mkdma(P, "scalar", dst[j * 128:(j + 1) * 128, :], xob[s][:], reads=[b_xob[s]]))
            return a_out_toks[-1]

        with nc.Block() as block:
            pre1 = sorted(ln_toks)
            if stage == 1:
                final = emit_ffn(1, x_in, out_t, pre_q1=pre1, psum_guard=pmisc_guard)
                st2q.close()
            elif stage == 2:
                ln2_toks = []
                emit_ffn(1, x_in, x1h, pre_q1=pre1, psum_guard=pmisc_guard,
                         post2_cb=lambda g, tk: ln2_toks.extend(
                             emit_ln2(x1h, range(g * 4, (g + 1) * 4), tk)))
                st2q.close()
                P.fence()
                final = [emit_attn(x1h, out_t, pre_ln2=ln2_toks)]
            else:
                ln2_toks = []
                emit_ffn(1, x_in, x1h, pre_q1=pre1, psum_guard=pmisc_guard,
                         post2_cb=lambda g, tk: ln2_toks.extend(
                             emit_ln2(x1h, range(g * 4, (g + 1) * 4), tk)))
                st2q.close()
                P.fence()
                emit_attn(x1h, x2h, pre_ln2=ln2_toks)
                P.fence()
                final = emit_ffn(2, x2h, out_t)
            P.finalize(nc, block, sems, final)
    return nc


def kernel(L0_attn_x, L0_attn_Wq, L0_attn_Wk, L0_attn_Wv, L0_attn_Wp, L0_attn_bp,
           L0_ffn_w_gate, L0_ffn_w_up, L0_ffn_w_down):
    x = np.ascontiguousarray(np.asarray(L0_attn_x, dtype=np.float32))
    B = x.shape[0]
    wq = np.ascontiguousarray(np.asarray(L0_attn_Wq, np.float32).reshape(C, C))
    wk = np.ascontiguousarray(np.asarray(L0_attn_Wk, np.float32).reshape(KV * D, C))
    wv = np.ascontiguousarray(np.asarray(L0_attn_Wv, np.float32).reshape(KV * D, C))
    wp = np.ascontiguousarray(np.asarray(L0_attn_Wp, np.float32))
    bp = np.ascontiguousarray(np.asarray(L0_attn_bp, np.float32).reshape(1, C))
    wg = np.ascontiguousarray(np.asarray(L0_ffn_w_gate, np.float32))
    wu = np.ascontiguousarray(np.asarray(L0_ffn_w_up, np.float32))
    wd = np.ascontiguousarray(np.asarray(L0_ffn_w_down, np.float32))
    ones = np.ones((128, 128), dtype=ml_dtypes.bfloat16)
    mask = np.triu(np.ones((128, 128))).astype(ml_dtypes.bfloat16)
    ident = np.eye(128, dtype=ml_dtypes.bfloat16)

    nc = build_program(STAGE)
    in_maps = []
    for b in range(B):
        in_maps.append({
            "x": np.ascontiguousarray(x[b]),
            "wq": wq, "wk": wk, "wv": wv, "wp": wp, "bp": bp,
            "wg": wg, "wu": wu, "wd": wd,
            "cones": ones, "cmask": mask, "cident": ident,
        })
    res = run_bass_kernel_spmd(nc, in_maps, core_ids=list(range(8)))
    out = np.stack([res.results[b]["out"] for b in range(B)], axis=0)
    return out
